# version 22
# baseline (speedup 1.0000x reference)
import numpy as np
import ml_dtypes
import contextlib

import concourse.bass as bass
import concourse.mybir as mybir
from concourse.bass_utils import run_bass_kernel_spmd

F32 = mybir.dt.float32
BF16 = mybir.dt.bfloat16
I32 = mybir.dt.int32
AF = mybir.ActivationFunctionType
ALU = mybir.AluOpType
AX = mybir.AxisListType

BFD = ml_dtypes.bfloat16

D = 2048
H = 16
HD = 128
M = 4
FF = 5632
S = 2048
NSEQ = 2
NKC = D // 128
NTC = S // 128
NFC = FF // 128
FT = 1024
NFST = S // FT
SCALE = float(HD) ** -0.5
EPS = 1e-6
LN_EPS = 1e-5

DO_ATTN = True
DO_FFN = True


def ap_of(buf):
    return buf.handle if hasattr(buf, "handle") else buf


_uid = [0]

def sb(es, nc, name, shape, dtype):
    _uid[0] += 1
    return es.enter_context(nc.sbuf_tensor(f"{name}_{_uid[0]}", shape, dtype))

def pt(es, nc, name, shape, dtype=F32):
    _uid[0] += 1
    return es.enter_context(nc.psum_tensor(f"{name}_{_uid[0]}", shape, dtype))


class Ctx:

    def __init__(self, nc, sems, dma_pool):
        self.nc = nc
        self.sems = sems
        self.ct = {k: 0 for k in sems}
        self.dma_pool = list(dma_pool)
        self.chan = {}
        self.settled = {}

    def _ch(self, channel):
        if channel not in self.chan:
            if not self.dma_pool:
                raise RuntimeError("out of DMA semaphores")
            self.chan[channel] = self.dma_pool.pop(0)
        return self.chan[channel]

    def dma(self, eng, dst, src, channel, burst=False):
        q = self._ch(channel)
        e = getattr(self.nc, eng)
        if not burst and self.ct[q] > self.settled.get(q, 0):
            e.wait_ge(self.sems[q], self.ct[q])
            self.settled[q] = self.ct[q]
        e.dma_start(dst, src).then_inc(self.sems[q], 16)
        self.ct[q] += 16
        return (q, self.ct[q])

    def mark(self, eng):
        n = {"vector": "dv", "scalar": "ac", "gpsimd": "po"}[eng]
        getattr(self.nc, eng).drain().then_inc(self.sems[n], 1)
        self.ct[n] += 1
        return (n, self.ct[n])

    def mark_mm(self, instr):
        instr.then_inc(self.sems["pe"], 1)
        self.ct["pe"] += 1
        return ("pe", self.ct["pe"])

    def waits(self, eng, toks):
        best = {}
        flat = []
        for t in toks:
            if t is None:
                continue
            if isinstance(t, list):
                flat.extend(x for x in t if x is not None)
            else:
                flat.append(t)
        for n, v in flat:
            best[n] = max(best.get(n, 0), v)
        for n, v in best.items():
            getattr(self.nc, eng).wait_ge(self.sems[n], v)

    def wait(self, eng, tok):
        self.waits(eng, [tok])

    def now(self, name):
        return (name, self.ct[name])

    def now_ch(self, channel):
        if channel not in self.chan:
            return None
        q = self.chan[channel]
        return (q, self.ct[q])

    def barrier(self):
        for ch, q in self.chan.items():
            if self.ct[q] > self.settled.get(q, 0):
                self.nc.sync.wait_ge(self.sems[q], self.ct[q])
                self.settled[q] = self.ct[q]
        bar = self.sems["bar"]
        base = self.ct["bar"]
        for eng in ("vector", "scalar", "gpsimd"):
            getattr(self.nc, eng).drain().then_inc(bar, 1)
        self.nc.tensor.nop().then_inc(bar, 1)
        self.nc.sync.nop().then_inc(bar, 1)
        self.ct["bar"] += 5
        for eng in ("vector", "scalar", "gpsimd", "tensor", "sync"):
            getattr(self.nc, eng).wait_ge(bar, base + 5)


def emit_ln_chunk(c, nchunks, io, bufs, toks, xsrc, col_of_kc, stage_w, sxn_psums):
    nc = io.nc
    v, s, t, g = nc.vector, nc.scalar, nc.tensor, nc.gpsimd
    xt = bufs["xtile"][c % 2]
    jA, jB, st1 = bufs["junkA"], bufs["junkB"], bufs["stat1"]
    xnt = bufs["xn_tm"][c % 2]
    tp = bufs["tpsum"]

    io.waits("sync", [toks.get(("xtile", c % 2))])
    ld = io.dma("sync", xt[:], xsrc, ("xt", c % 2))
    o = 8 * (c % 2)
    io.waits("scalar", [ld, toks.get(("st1war", c % 2))])
    s.activation(jA[:], xt[:], AF.Square, bias=0.0, scale=1.0,
                 accum_out=st1[:, o + 1:o + 2])
    s.activation(jB[:, 1024:2048], xt[:, 0:1024], AF.Identity, bias=0.0, scale=1.0,
                 accum_out=st1[:, o + 7:o + 8])
    tac = io.mark("scalar")
    io.wait("vector", ld)
    v.tensor_scalar(jB[:, 0:1024], xt[:, 1024:2048], 0.0, 0.0, ALU.add, ALU.add,
                    accum_out=st1[:, o:o + 1])
    io.wait("vector", tac)
    v.drain()
    v.tensor_tensor(st1[:, o + 6:o + 7], st1[:, o:o + 1], st1[:, o + 7:o + 8], ALU.add)
    v.drain()
    v.tensor_scalar_mul(st1[:, o + 2:o + 3], st1[:, o + 6:o + 7], 1.0 / 2048.0)
    v.drain()
    v.tensor_mul(st1[:, o + 3:o + 4], st1[:, o + 2:o + 3], st1[:, o + 2:o + 3])
    io.wait("vector", tac)
    v.drain()
    v.scalar_tensor_tensor(st1[:, o + 4:o + 5], st1[:, o + 1:o + 2], 1.0 / 2048.0,
                           st1[:, o + 3:o + 4], ALU.mult, ALU.subtract)
    tvar = io.mark("vector")
    toks[("st1war", c % 2)] = tvar
    io.wait("scalar", tvar)
    s.activation(st1[:, o + 6:o + 7], st1[:, o + 4:o + 5], AF.Ln,
                 bias=bufs["epsb"][:], scale=1.0)
    s.drain()
    s.activation(st1[:, o + 5:o + 6], st1[:, o + 6:o + 7], AF.Exp,
                 bias=0.0, scale=-0.5)
    trstd = io.mark("scalar")
    io.waits("vector", [trstd, toks.get(("xn_tm", c % 2))])
    v.tensor_scalar(xnt[:], xt[:], st1[:, o + 2:o + 3], st1[:, o + 5:o + 6],
                    ALU.subtract, ALU.mult)
    tnorm = io.mark("vector")
    toks[("xtile", c % 2)] = tnorm
    for grp in range(4):
        pg = tp[grp % 2]
        io.waits("tensor", [tnorm, toks.get(("tpsum", grp % 2))])
        for j in range(4):
            kc = grp * 4 + j
            mm = t.transpose(pg[:, 128 * j:128 * (j + 1)],
                             xnt[:, 128 * kc:128 * (kc + 1)], bufs["ident"][:])
        tpe = io.mark_mm(mm)
        eng = "vector" if grp % 2 == 0 else "scalar"
        io.wait(eng, tpe)
        dst = bass.AP(ap_of(bufs["xn"]), col_of_kc(grp * 4),
                      [[bufs["xn_psz"], 128], [stage_w, 4], [1, 128]])
        if eng == "vector":
            v.tensor_copy(dst, pg[:])
        else:
            s.copy(dst, pg[:])
        toks[("tpsum", grp % 2)] = io.mark(eng)
    if sxn_psums is not None:
        for q4 in range(4):
            mm = t.matmul(sxn_psums[q4][:], bufs["ones_bf"][:],
                          xnt[:, 512 * q4:512 * (q4 + 1)],
                          start=(c == 0), stop=(c == nchunks - 1))
        if c == nchunks - 1:
            toks["sxn"] = io.mark_mm(mm)
        toks[("xn_tm", c % 2)] = ("pe", io.ct["pe"] + (0 if c == nchunks - 1 else 1))
        if c < nchunks - 1:
            io.mark_mm(mm)
    else:
        toks[("xn_tm", c % 2)] = toks[("tpsum", 1)]


def emit_attn_seq(io, toks, seq, T):
    nc = io.nc
    t, v, s, g, y = nc.tensor, nc.vector, nc.scalar, nc.gpsimd, nc.sync
    xs, ps, os_ = T["xs"], T["ps"], T["os"]
    r0 = seq * S

    es0 = contextlib.ExitStack()
    with es0:
        xn = sb(es0, nc, "xn_a", [128, NKC * S], BF16)
        mix = sb(es0, nc, "mix", [128, NTC * D], BF16)
        xbarT = sb(es0, nc, "xbarT", [128, NKC], BF16)
        wppb = sb(es0, nc, "wppb", [128, M], F32)
        es1 = contextlib.ExitStack()
        with es1:
            xtA = sb(es1, nc, "xtileA", [128, D], F32)
            xtB = sb(es1, nc, "xtileB", [128, D], F32)
            jA = sb(es1, nc, "junkA", [128, D], F32)
            jB = sb(es1, nc, "junkB", [128, D], F32)
            st1 = sb(es1, nc, "stat1", [128, 16], F32)
            xnA = sb(es1, nc, "xn_tmA", [128, D], BF16)
            xnB = sb(es1, nc, "xn_tmB", [128, D], BF16)
            sxn_sb = sb(es1, nc, "sxn_sb", [1, D], F32)
            tpA = pt(es1, nc, "tpA", [128, 512], BF16)
            tpB = pt(es1, nc, "tpB", [128, 512], BF16)
            sx0 = pt(es1, nc, "sx0", [1, 512])
            sx1 = pt(es1, nc, "sx1", [1, 512])
            sx2 = pt(es1, nc, "sx2", [1, 512])
            sx3 = pt(es1, nc, "sx3", [1, 512])
            pxb = pt(es1, nc, "pxb", [128, 16])
            bufs = dict(T)
            bufs.update(xn=xn, xn_psz=NKC * S, xtile=[xtA, xtB], junkA=jA, junkB=jB,
                        stat1=st1, xn_tm=[xnA, xnB], tpsum=[tpA, tpB])
            sxp = [sx0, sx1, sx2, sx3]
            for c in range(NTC):
                emit_ln_chunk(c, NTC, io, bufs, toks, xs[r0 + 128 * c:r0 + 128 * (c + 1), :],
                              lambda kc, c=c: kc * S + 128 * c, S, sxp)
                T["cv_step"](1)
            io.wait("vector", toks["sxn"])
            for q4 in range(4):
                v.tensor_copy(sxn_sb[:, 512 * q4:512 * (q4 + 1)], sxp[q4][:])
            tsxe = io.mark("vector")
            io.wait("tensor", tsxe)
            for kc in range(NKC):
                mm = t.matmul(pxb[:, kc:kc + 1], sxn_sb[0:1, 128 * kc:128 * (kc + 1)],
                              T["ones1_f"][:], start=True, stop=True)
            tpe = io.mark_mm(mm)
            io.wait("vector", tpe)
            v.tensor_scalar_mul(xbarT[:], pxb[:], 1.0 / 2048.0)
            toks["xbarT"] = io.mark("vector")
            io.waits("sync", [io.now("dv"), io.now("ac")])
            toks[("xnf", seq)] = io.dma(
                "sync", bass.AP(T["xnF"], seq * D * S, [[S, 128], [128 * S, NKC], [1, S]]),
                xn[:].rearrange("p (c t) -> p c t", c=NKC), "xnf")
        io.barrier()

        es2 = contextlib.ExitStack()
        with es2:
            wq0 = sb(es2, nc, "wq0", [128, NKC * 128], BF16)
            wq1 = sb(es2, nc, "wq1", [128, NKC * 128], BF16)
            wk0 = sb(es2, nc, "wk0", [128, NKC * 128], BF16)
            wv0 = sb(es2, nc, "wv0", [128, NKC * 128], BF16)
            mem0 = sb(es2, nc, "mem0", [128, M * 128], BF16)
            mem1 = sb(es2, nc, "mem1", [128, M * 128], BF16)
            mem2 = sb(es2, nc, "mem2", [128, M * 128], BF16)
            sqb_ = [sb(es2, nc, f"sq{i}", [128, S], BF16) for i in range(2)]
            skb_ = [sb(es2, nc, f"sk{i}", [128, S], BF16) for i in range(2)]
            skt0 = sb(es2, nc, "sktm0", [128, S], BF16)
            sktb = [skt0, skt0]
            vtmb = [sb(es2, nc, f"vtm{i}", [128, S], BF16) for i in range(2)]
            Sball = [sb(es2, nc, f"Sball{i}", [128, NTC * 133], BF16) for i in range(2)]
            wppbb = [sb(es2, nc, f"wppb{i}", [128, 8], F32) for i in range(2)]
            vst0 = sb(es2, nc, "vst0", [128, 512], BF16)
            vst1 = sb(es2, nc, "vst1", [128, 512], BF16)
            vstb = [vst0, vst1]
            el0 = sb(es2, nc, "el0", [128, 512], BF16)
            el1 = sb(es2, nc, "el1", [128, 512], BF16)
            el2 = sb(es2, nc, "el2", [128, 512], BF16)
            el3 = sb(es2, nc, "el3", [128, 512], BF16)
            msc0 = sb(es2, nc, "msc0", [128, 128], BF16)
            msc1 = sb(es2, nc, "msc1", [128, 128], BF16)
            mm640a = sb(es2, nc, "mm640a", [128, 640], F32)
            mm640b = sb(es2, nc, "mm640b", [128, 640], F32)
            den5 = sb(es2, nc, "den5", [128, 8], F32)
            betab = sb(es2, nc, "betab", [128, 8], F32)
            qbarb = sb(es2, nc, "qbarb", [128, 1], BF16)
            pA = pt(es2, nc, "pA", [128, 1024])
            pKV = pt(es2, nc, "pKV", [128, 129])
            pU = pt(es2, nc, "pU", [128, 512])
            pDNa = pt(es2, nc, "pDNa", [128, 136])
            pDNb = pt(es2, nc, "pDNb", [128, 136])
            pSC2 = pt(es2, nc, "pSC2", [128, 256])
            pD = pt(es2, nc, "pD", [128, 512], BF16)
            wqb, wkb, wvb = [wq0, wq1], [wk0, wk0], [wv0, wv0]
            memb = [mem0, mem1, mem2]
            mscb = [msc0, msc1]
            m6b = [mm640a, mm640b]
            rl = T["smp"]

            def issue_head_weights(h):
                p, p3 = h % 2, h % 3
                wt_toks = []
                io.waits("gpsimd", [toks.get(("wslab", p)), toks.get(("memslab", p3))])
                for wi, (wt, bb, ch) in enumerate(((T["wqT"], wqb[p], ("wh", p)),
                                                   (T["wkT"], wkb[p], "whk"),
                                                   (T["wvT"], wvb[p], "whv"))):
                    src = bass.AP(wt, 128 * h, [[D, 128], [128 * D, NKC], [1, 128]])
                    dst = bb[:].rearrange("p (c e) -> p c e", c=NKC)
                    wt_toks.append(io.dma("gpsimd", dst, src, ch))
                wt_toks.append(io.dma("gpsimd",
                                      memb[p3][:].rearrange("p (m e) -> p m e", m=M),
                                      T["memT"].ap()[h], ("mem", p3)))
                return wt_toks

            def sigma_evac(blk, dst):
                tb = blk % 2
                ea = el0 if tb == 0 else el1
                eb = el2 if tb == 0 else el3
                col = 512 * blk
                tpe = toks[("projmm", tb)]
                io.waits("scalar", [tpe, toks.get(("el", tb))])
                s.activation(ea[:], pA[:, 512 * tb:512 * (tb + 1)], AF.Exp,
                             bias=0.0, scale=1.0)
                s.activation(eb[:], pA[:, 512 * tb:512 * (tb + 1)], AF.Relu,
                             bias=0.0, scale=1.0)
                tac = io.mark("scalar")
                io.wait("vector", tac)
                v.scalar_tensor_tensor(dst[:, col:col + 512], ea[:], 1.0,
                                       eb[:], ALU.min, ALU.add)
                tdv = io.mark("vector")
                toks[("pA", tb)] = tac
                toks[("el", tb)] = tdv
                return tdv

            pbq = []

            def pb_step(n=1):
                T["cv_step"](2)
                for _ in range(n):
                    if pbq:
                        pbq.pop(0)()

            pSCb_ = [pSC2[:, 0:128], pSC2[:, 128:256]]
            pDNp = [pDNa, pDNb]

            def make_phaseB(hh, twh):
                ph, p3 = hh % 2, hh % 3
                sq, sk = sqb_[ph], skb_[ph]
                v_tm = vtmb[ph]
                Sb_all = Sball[ph]
                wppb = wppbb[ph]

                def first():
                    io.waits("tensor", [toks[("sq", ph)], toks[("sk", ph)],
                                        toks.get(("pSC_free", 0))])
                    mm = t.matmul(pSCb_[0], sk[:, 0:128], sq[:, 0:128],
                                  start=True, stop=True)
                    toks[("t_sc", 0)] = io.mark_mm(mm)
                    io.waits("vector", [toks[("t_sc", 0)], toks.get(("msc_war", 0))])
                    v.tensor_tensor(mscb[0][:], pSCb_[0], T["mask"][:], ALU.mult)
                    tA = io.mark("vector")
                    toks[("msk", 0)] = tA
                    toks[("pSC_free", 0)] = tA

                def chunk(c):
                    col = 128 * c
                    mp = mscb[c % 2]
                    m6 = m6b[c % 2]
                    pDN = pDNp[c % 2]
                    io.waits("tensor", [toks[("msk", c)], toks.get(("pDN_free", c % 2)),
                                        toks[("Sslot", ph, c)]])
                    t.matmul(pDN[:, 0:5], sq[:, col:col + 128],
                             Sb_all[:, 133 * c + 128:133 * c + 133], start=True, stop=False)
                    t.matmul(pDN[:, 0:1], mp[:], T["ones_bf"][:], start=False, stop=False)
                    t.matmul(pDN[:, 8:136], sq[:, col:col + 128],
                             Sb_all[:, 133 * c:133 * c + 128], start=False, stop=False)
                    mm = t.matmul(pDN[:, 8:136], mp[:], v_tm[:, col:col + 128],
                                  start=False, stop=True)
                    t_mms = io.mark_mm(mm)
                    toks[("msc_war", c % 2)] = t_mms
                    if c + 1 < NTC:
                        io.waits("tensor", [toks.get(("pSC_free", (c + 1) % 2))])
                        mm = t.matmul(pSCb_[(c + 1) % 2], sk[:, col + 128:col + 256],
                                      sq[:, col + 128:col + 256], start=True, stop=True)
                        toks[("t_sc", c + 1)] = io.mark_mm(mm)
                        mpn = mscb[(c + 1) % 2]
                        io.waits("vector", [toks[("t_sc", c + 1)],
                                            toks.get(("msc_war", (c + 1) % 2))])
                        v.tensor_tensor(mpn[:], pSCb_[(c + 1) % 2], T["mask"][:], ALU.mult)
                        tA = io.mark("vector")
                        toks[("msk", c + 1)] = tA
                        toks[("pSC_free", (c + 1) % 2)] = tA
                    io.waits("tensor", [twh[3], toks.get("pU_free")])
                    mm = t.matmul(pU[:], sq[:, col:col + 128], memb[p3][:],
                                  start=True, stop=True)
                    t_u = io.mark_mm(mm)
                    if c + 1 == NTC:
                        toks[("memslab", p3)] = t_u
                    io.waits("vector", [t_mms, toks.get(("m6_war", c % 2))])
                    v.tensor_scalar_max(den5[:, 0:5], pDN[:, 0:5], EPS)
                    v.drain()
                    v.reciprocal(den5[:, 0:5], den5[:, 0:5])
                    v.drain()
                    v.tensor_mul(betab[:, 0:5], wppb[:, 0:5], den5[:, 0:5])
                    v.drain()
                    io.wait("vector", t_u)
                    bcast = bass.AP(ap_of(betab), 1, [[8, 128], [1, 4], [0, 128]])
                    v.tensor_tensor(m6[:, 0:512], pU[:], bcast, ALU.mult)
                    tB0 = io.mark("vector")
                    toks["pU_free"] = tB0
                    v.tensor_scalar_mul(m6[:, 512:640], pDN[:, 8:136], betab[:, 0:1])
                    tn = io.mark("vector")
                    toks[("pDN_free", c % 2)] = tn
                    with nc.allow_low_precision(reason="fp32-accumulated reduce, bf16 store"):
                        v.tensor_reduce(mix[:, c * D + 128 * hh:c * D + 128 * (hh + 1)],
                                        bass.AP(ap_of(m6), 0, [[640, 128], [1, 128], [128, 5]]),
                                        AX.X, ALU.add)
                    tB = io.mark("vector")
                    toks[("m6_war", c % 2)] = tB
                    toks["mix_done"] = tB

                out = [first]
                for c in range(NTC):
                    out.append(lambda c=c: chunk(c))
                return out

            def emit_wpp(h):
                p = h % 2
                io.waits("tensor", tw[h] + [toks.get("xbarT"), toks.get("pKV_free")])
                for kc in range(NKC):
                    mm = t.matmul(pKV[:, 0:1], wqb[p][:, 128 * kc:128 * (kc + 1)],
                                  xbarT[:, kc:kc + 1], start=(kc == 0), stop=(kc == NKC - 1))
                tpe = io.mark_mm(mm)
                io.wait("scalar", tpe)
                s.activation(qbarb[:], pKV[:, 0:1], AF.Copy, bias=0.0, scale=1.0)
                tqb = io.mark("scalar")
                pb_step()
                io.waits("tensor", [tqb])
                mm = t.matmul(pKV[0:1, 4:8], qbarb[:], T["mnb16"][:, M * h:M * (h + 1)],
                              start=True, stop=True)
                tpe = io.mark_mm(mm)
                io.wait("vector", tpe)
                v.tensor_scalar_mul(rl[0:1, 80:84], pKV[0:1, 4:8], SCALE)
                v.drain()
                v.tensor_reduce(rl[0:1, 84:85], rl[0:1, 80:84], AX.X, ALU.max)
                v.drain()
                v.tensor_scalar_mul(rl[0:1, 85:86], rl[0:1, 84:85], -1.0)
                tdv = io.mark("vector")
                io.wait("scalar", tdv)
                s.activation(rl[0:1, 88:92], rl[0:1, 80:84], AF.Exp,
                             bias=rl[0:1, 85:86], scale=1.0)
                tac = io.mark("scalar")
                io.wait("vector", tac)
                v.tensor_mul(rl[0:1, 88:92], rl[0:1, 88:92], T["ne4"][:])
                v.drain()
                v.tensor_reduce(rl[0:1, 92:93], rl[0:1, 88:92], AX.X, ALU.add)
                v.drain()
                v.tensor_scalar_max(rl[0:1, 93:94], rl[0:1, 92:93], 1e-30)
                v.drain()
                v.reciprocal(rl[0:1, 94:95], rl[0:1, 93:94])
                v.drain()
                v.tensor_scalar(rl[0:1, 96:100], rl[0:1, 88:92], rl[0:1, 94:95],
                                T["gsig"][0:1, h:h + 1], ALU.mult, ALU.mult)
                v.tensor_copy(rl[0:1, 95:96], rl[0:1, h:h + 1])
                tdv = io.mark("vector")
                pb_step()
                io.waits("tensor", [tdv])
                mm = t.matmul(pKV[:, 8:13], T["onesr_f"][:], rl[0:1, 95:100],
                              start=True, stop=True)
                tpe = io.mark_mm(mm)
                io.wait("vector", tpe)
                v.tensor_copy(wppbb[p][:, 0:5], pKV[:, 8:13])
                toks["pKV_free"] = io.mark("vector")

            tw = {0: issue_head_weights(0)}
            emit_wpp(0)
            for h in range(H):
                p, p3 = h % 2, h % 3
                sq, sk = sqb_[p], skb_[p]
                sk_tm, v_tm = sktb[p], vtmb[p]
                Sb_all = Sball[p]

                for blk in range(4):
                    tb = blk % 2
                    io.waits("tensor", [toks.get(("pA", tb))])
                    for kc in range(NKC):
                        mm = t.matmul(pA[:, 512 * tb:512 * (tb + 1)],
                                      wkb[p][:, 128 * kc:128 * (kc + 1)],
                                      xn[:, kc * S + 512 * blk:kc * S + 512 * (blk + 1)],
                                      start=(kc == 0), stop=(kc == NKC - 1))
                    toks[("projmm", tb)] = io.mark_mm(mm)
                    pb_step()
                    tdv = sigma_evac(blk, sk)
                toks[("sk", p)] = tdv
                for blk in range(4):
                    tb = blk % 2
                    vs = vstb[tb]
                    io.waits("tensor", [toks.get(("pA", tb))])
                    for kc in range(NKC):
                        mm = t.matmul(pA[:, 512 * tb:512 * (tb + 1)],
                                      wvb[p][:, 128 * kc:128 * (kc + 1)],
                                      xn[:, kc * S + 512 * blk:kc * S + 512 * (blk + 1)],
                                      start=(kc == 0), stop=(kc == NKC - 1))
                    tpe = io.mark_mm(mm)
                    pb_step()
                    io.waits("scalar", [tpe, toks.get(("vst", tb))])
                    s.copy(vs[:], pA[:, 512 * tb:512 * (tb + 1)])
                    tac = io.mark("scalar")
                    toks[("pA", tb)] = tac
                    io.waits("tensor", [tac, toks.get("pD_ev")])
                    for j in range(4):
                        mm = t.transpose(pD[:, 128 * j:128 * (j + 1)],
                                         vs[:, 128 * j:128 * (j + 1)], T["ident"][:])
                    tpe = io.mark_mm(mm)
                    toks[("vst", tb)] = tpe
                    io.wait("vector", tpe)
                    v.tensor_copy(v_tm[:, 512 * blk:512 * (blk + 1)], pD[:])
                    toks["pD_ev"] = io.mark("vector")
                for q4 in range(4):
                    io.waits("tensor", [toks[("sk", p)], toks.get("pD_ev")])
                    for j in range(4):
                        c = 4 * q4 + j
                        mm = t.transpose(pD[:, 128 * j:128 * (j + 1)],
                                         sk[:, 128 * c:128 * (c + 1)], T["ident"][:])
                    tpe = io.mark_mm(mm)
                    if q4 % 2 == 1:
                        pb_step()
                    eng = "vector" if q4 % 2 == 0 else "scalar"
                    io.wait(eng, tpe)
                    if q4 % 2 == 0:
                        v.tensor_copy(sk_tm[:, 512 * q4:512 * (q4 + 1)], pD[:])
                    else:
                        s.copy(sk_tm[:, 512 * q4:512 * (q4 + 1)], pD[:])
                    toks["pD_ev"] = io.mark(eng)
                tvtm = toks["pD_ev"]

                io.waits("gpsimd", [io.now("pe"), io.now("dv"), io.now("ac")])
                g.memset(Sb_all[:, 0:129], 0.0)
                src_mn = bass.AP(ap_of(T["mnb16"]), 4 * h, [[64, 128], [0, NTC], [1, 4]])
                dst_mn = bass.AP(ap_of(Sb_all), 129, [[NTC * 133, 128], [133, NTC], [1, 4]])
                g.tensor_copy(dst_mn, src_mn)
                tSi = io.mark("gpsimd")
                toks[("Sslot", p, 0)] = tSi
                toks[("Scp", -1)] = tSi

                if h + 1 < H:
                    toks[("wslab", (h + 1) % 2)] = io.now("pe")
                    tw[h + 1] = issue_head_weights(h + 1)

                for blk in range(4):
                    tb = blk % 2
                    kv_cs = [c for c in range(4 * blk, 4 * blk + 4) if c < NTC - 1]
                    io.waits("tensor", [toks.get(("pA", tb))])
                    for kc in range(NKC):
                        mm = t.matmul(pA[:, 512 * tb:512 * (tb + 1)],
                                      wqb[p][:, 128 * kc:128 * (kc + 1)],
                                      xn[:, kc * S + 512 * blk:kc * S + 512 * (blk + 1)],
                                      start=(kc == 0), stop=(kc == NKC - 1))
                        if kc == NKC - 1:
                            toks[("projmm", tb)] = io.mark_mm(mm)
                        if kc % 4 == 3 and kv_cs:
                            c = kv_cs.pop(0)
                            col = 128 * c
                            io.waits("tensor", [toks[("Scp", c - 1)]])
                            t.matmul(pKV[:, 0:128], sk_tm[:, col:col + 128],
                                     v_tm[:, col:col + 128], start=(c == 0), stop=False)
                            mm2 = t.matmul(pKV[:, 128:129], sk_tm[:, col:col + 128],
                                           T["ones_bf"][:], start=False,
                                           stop=(c == NTC - 2))
                            tkv = io.mark_mm(mm2)
                            io.waits("scalar", [tkv, toks.get(("Sslot", p, c - 1))])
                            s.copy(Sb_all[:, 133 * (c + 1):133 * (c + 1) + 129], pKV[:])
                            tcp = io.mark("scalar")
                            toks[("Scp", c)] = tcp
                            toks[("Sslot", p, c + 1)] = tcp
                    pb_step()
                    tdv = sigma_evac(blk, sq)
                toks[("sq", p)] = tdv
                toks["pKV_free"] = toks[("Scp", NTC - 2)]

                pb_step(NTC + 1)
                pbq = make_phaseB(h, tw[h])
                if h + 1 < H:
                    emit_wpp(h + 1)
            pb_step(NTC + 1)
        io.barrier()
        io.barrier()

        es3 = contextlib.ExitStack()
        with es3:
            woeb = [sb(es3, nc, f"woeb{i}", [128, NKC * 512], BF16) for i in range(2)]
            ot0 = sb(es3, nc, "otile0", [128, 512], F32)
            ot1 = sb(es3, nc, "otile1", [128, 512], F32)
            xe0 = sb(es3, nc, "xev0", [128, 512], F32)
            xe1 = sb(es3, nc, "xev1", [128, 512], F32)
            gA = pt(es3, nc, "gA", [128, 256], BF16)
            gB = pt(es3, nc, "gB", [128, 256], BF16)
            wo0 = pt(es3, nc, "wo0", [128, 512])
            wo1 = pt(es3, nc, "wo1", [128, 512])
            pairs = [(a, b) for a in range(NTC) for b in range(a, NKC)]
            gp = [gA, gB]
            for i, (a, b) in enumerate(pairs):
                pg = gp[i % 2]
                io.waits("tensor", [toks.get(("gT", i % 2)), toks.get("mix_done")])
                mm = t.transpose(pg[:, 0:128], mix[:, a * D + 128 * b:a * D + 128 * (b + 1)],
                                 T["ident"][:])
                if b != a:
                    mm = t.transpose(pg[:, 128:256],
                                     mix[:, b * D + 128 * a:b * D + 128 * (a + 1)],
                                     T["ident"][:])
                tpe = io.mark_mm(mm)
                io.wait("vector", tpe)
                v.tensor_copy(mix[:, b * D + 128 * a:b * D + 128 * (a + 1)], pg[:, 0:128])
                if b != a:
                    v.tensor_copy(mix[:, a * D + 128 * b:a * D + 128 * (b + 1)], pg[:, 128:256])
                toks[("gT", i % 2)] = io.mark("vector")
            tgt = toks[("gT", (len(pairs) - 1) % 2)]
            wop, oti, xei = [wo0, wo1], [ot0, ot1], [xe0, xe1]

            def issue_wo(eb):
                src = bass.AP(T["woT"], 512 * eb, [[D, 128], [128 * D, NKC], [1, 512]])
                io.waits("gpsimd", [toks.get(("woslot", eb % 2))])
                toks[("wold", eb)] = io.dma(
                    "gpsimd", woeb[eb % 2][:].rearrange("p (c e) -> p c e", c=NKC),
                    src, ("wo", eb % 2))

            issue_wo(0)
            issue_wo(1)
            for eb in range(4):
                for tc in range(NTC):
                    i = tc % 2
                    io.waits("tensor", [toks[("wold", eb)], tgt, toks.get(("wo_ev", i))])
                    for j in range(NKC):
                        mm = t.matmul(wop[i][:], mix[:, j * D + 128 * tc:j * D + 128 * (tc + 1)],
                                      woeb[eb % 2][:, 512 * j:512 * (j + 1)],
                                      start=(j == 0), stop=(j == NKC - 1))
                    tpe = io.mark_mm(mm)
                    if DO_FFN:
                        io.waits("scalar", [tpe, toks.get(("otile", i))])
                        s.activation(oti[i][:], wop[i][:], AF.Copy, bias=0.0, scale=1.0)
                        tev = io.mark("scalar")
                        dstt = ps
                    else:
                        io.waits("sync", [toks.get(("xev", i))])
                        tx = io.dma("sync", xei[i][:],
                                    xs[r0 + 128 * tc:r0 + 128 * (tc + 1), 512 * eb:512 * (eb + 1)],
                                    ("xe", i))
                        io.waits("vector", [tpe, tx, toks.get(("otile", i))])
                        v.tensor_tensor(oti[i][:], wop[i][:], xei[i][:], ALU.add)
                        tev = io.mark("vector")
                        toks[("xev", i)] = tev
                        dstt = os_
                    toks[("wo_ev", i)] = tev
                    io.wait("sync", tev)
                    toks[("otile", i)] = io.dma(
                        "sync", dstt[r0 + 128 * tc:r0 + 128 * (tc + 1), 512 * eb:512 * (eb + 1)],
                        oti[i][:], ("ot", i))
                toks[("woslot", eb % 2)] = io.now("pe")
                if eb + 2 < 4:
                    issue_wo(eb + 2)
            toks[("partial_w", seq)] = [io.now_ch(("ot", 0)), io.now_ch(("ot", 1))]
    io.barrier()


def emit_ffn_seq(io, toks, seq, T):
    nc = io.nc
    t, v, s, g, y = nc.tensor, nc.vector, nc.scalar, nc.gpsimd, nc.sync
    xs, ps, os_ = T["xs"], T["ps"], T["os"]
    NH1 = 22

    esq = contextlib.ExitStack()
    with esq:
        xn = sb(esq, nc, "xn_f", [128, NKC * FT], BF16)
        wgs = [sb(esq, nc, f"wgs{i}", [128, NKC * 128], BF16) for i in range(2)]
        wus = [sb(esq, nc, f"wus{i}", [128, NKC * 128], BF16) for i in range(2)]

        def issue_fc(fc):
            p = fc % 2
            io.waits("gpsimd", [toks.get(("wgu", p)), T["tffnw"]])
            a = io.dma("gpsimd", wgs[p][:].rearrange("p (c f) -> p c f", c=NKC),
                       bass.AP(T["wgB"], 128 * fc,
                               [[FF, 128], [128 * FF, NKC], [1, 128]]), ("wg", p))
            b = io.dma("gpsimd", wus[p][:].rearrange("p (c f) -> p c f", c=NKC),
                       bass.AP(T["wuB"], 128 * fc,
                               [[FF, 128], [128 * FF, NKC], [1, 128]]), ("wg", p),
                       burst=True)
            return [a, b]

        def load_xn(stg, war=None):
            xh = NKC // 2
            io.waits("sync", [toks[("xnf", seq)], toks.get(("xnst", 0)), war])
            r = [io.dma("sync", xn[:, 0:xh * FT].rearrange("p (c t) -> p c t", c=xh),
                        bass.AP(T["xnF"], seq * D * S + stg * FT,
                                [[S, 128], [128 * S, xh], [1, FT]]), ("xnst", 0))]
            io.waits("scalar", [toks[("xnf", seq)], toks.get(("xnst", 1)), war])
            r.append(io.dma("scalar",
                            xn[:, xh * FT:].rearrange("p (c t) -> p c t", c=xh),
                            bass.AP(T["xnF"], seq * D * S + xh * 128 * S + stg * FT,
                                    [[S, 128], [128 * S, xh], [1, FT]]), ("xnst", 1)))
            toks[("xnst", 0)] = r[0]
            toks[("xnst", 1)] = r[1]
            return r

        pre = {"t_xn": load_xn(0), "twg0": issue_fc(0)}
        for stg in range(NFST):
            t0 = seq * S + stg * FT
            esh = contextlib.ExitStack()
            emit_ffn_stage(io, toks, seq, T, stg, t0, esh, nc, xn, wgs, wus,
                           issue_fc, load_xn, pre)
    io.barrier()


def emit_ffn_stage(io, toks, seq, T, stg, t0, esh, nc, xn, wgs, wus,
                   issue_fc, load_xn, pre):
    t, v, s, g, y = nc.tensor, nc.vector, nc.scalar, nc.gpsimd, nc.sync
    xs, ps, os_ = T["xs"], T["ps"], T["os"]
    NH1 = 22
    if True:
        with esh:
            hh = sb(esh, nc, "hh", [128, NFC * FT], BF16)
            wdh = [sb(esh, nc, f"wdh{i}", [128, NH1 * 512], BF16) for i in range(2)]

            def issue_wd(g):
                eb, hf = g // 2, g % 2
                sl = g % 2
                io.waits("gpsimd", [toks.get(("wdslot", sl)), T["tffnw"]])
                toks[("wdld", g)] = io.dma(
                    "gpsimd", wdh[sl][:].rearrange("p (c e) -> p c e", c=NH1),
                    bass.AP(T["wdB"], 512 * eb + NH1 * hf * 128 * D,
                            [[D, 128], [128 * D, NH1], [1, 512]]), ("wd", sl))

            esl = contextlib.ExitStack()
            with esl:
                esg = contextlib.ExitStack()
                with esg:
                    gts0 = sb(esg, nc, "gts0", [128, 512], F32)
                    gts1 = sb(esg, nc, "gts1", [128, 512], F32)
                    pg0 = pt(esg, nc, "pg0", [128, 512])
                    pg1 = pt(esg, nc, "pg1", [128, 512])
                    pu0 = pt(esg, nc, "pu0", [128, 512])
                    pu1 = pt(esg, nc, "pu1", [128, 512])
                    gts, pgs, pus = [gts0, gts1], [pg0, pg1], [pu0, pu1]

                    twg = {0: pre["twg0"]}
                    issue_wd(0)
                    issue_wd(1)
                    io.waits("tensor", pre["t_xn"])
                    tdv = None
                    for fc in range(NFC):
                        p = fc % 2
                        if fc + 1 < NFC:
                            toks[("wgu", (fc + 1) % 2)] = io.now("pe")
                            twg[fc + 1] = issue_fc(fc + 1)
                        for tb in range(FT // 512):
                            io.waits("tensor", twg[fc] + [toks.get(("pgu", tb))])
                            for kc in range(NKC):
                                t.matmul(pgs[tb][:], wgs[p][:, 128 * kc:128 * (kc + 1)],
                                         xn[:, kc * FT + 512 * tb:kc * FT + 512 * (tb + 1)],
                                         start=(kc == 0), stop=(kc == NKC - 1))
                            for kc in range(NKC):
                                mm = t.matmul(pus[tb][:], wus[p][:, 128 * kc:128 * (kc + 1)],
                                              xn[:, kc * FT + 512 * tb:kc * FT + 512 * (tb + 1)],
                                              start=(kc == 0), stop=(kc == NKC - 1))
                            tpe = io.mark_mm(mm)
                            io.waits("scalar", [tpe, toks.get(("gts", tb))])
                            s.activation(gts[tb][:], pgs[tb][:], AF.Silu, bias=0.0,
                                         scale=T["sgc"][:, fc:fc + 1])
                            tac = io.mark("scalar")
                            io.wait("vector", tac)
                            v.scalar_tensor_tensor(
                                hh[:, fc * FT + 512 * tb:fc * FT + 512 * (tb + 1)],
                                pus[tb][:], T["suc"][:, fc:fc + 1], gts[tb][:],
                                ALU.mult, ALU.mult)
                            tdv = io.mark("vector")
                            toks[("pgu", tb)] = tdv
                            toks[("gts", tb)] = tdv
                    thh = tdv
            io.barrier()

            esd = contextlib.ExitStack()
            with esd:
                pt0 = sb(esd, nc, "ptile0", [128, 512], F32)
                pt1 = sb(esd, nc, "ptile1", [128, 512], F32)
                xe0 = sb(esd, nc, "xev0", [128, 512], F32)
                xe1 = sb(esd, nc, "xev1", [128, 512], F32)
                ot0 = sb(esd, nc, "otile0", [128, 512], F32)
                ot1 = sb(esd, nc, "otile1", [128, 512], F32)
                sdrow = sb(esd, nc, "sdrow", [1, 512], F32)
                sdb = sb(esd, nc, "sdb", [128, 512], F32)
                NTCS = FT // 128
                pds = [pt(esd, nc, f"pdt{i}", [128, 512]) for i in range(NTCS)]
                pts, xes, ots = [pt0, pt1], [xe0, xe1], [ot0, ot1]

                def bcast_sd(eb):
                    io.waits("sync", [toks.get("sdrow_w")])
                    tsd = io.dma("sync", sdrow[:],
                                 T["sdi"].ap()[0:1, 512 * eb:512 * (eb + 1)], "consts")
                    io.waits("tensor", [tsd, toks.get("sdb_w"), io.now("dv")])
                    mm = t.matmul(pds[0][:], T["onesr_f"][:], sdrow[0:1, :],
                                  start=True, stop=True)
                    tpe = io.mark_mm(mm)
                    toks["sdrow_w"] = tpe
                    io.wait("vector", tpe)
                    v.tensor_copy(sdb[:], pds[0][:])
                    tsdb = io.mark("vector")
                    io.wait("tensor", tsdb)
                    return tsdb
                if stg + 1 < NFST:
                    toks[("wgu", 0)] = io.now("pe")
                    toks[("wgu", 1)] = io.now("pe")
                    pre["t_xn"] = load_xn(stg + 1, war=thh)
                    pre["twg0"] = issue_fc(0)
                for eb in range(4):
                    tsdb = bcast_sd(eb)
                    io.waits("tensor", [toks[("wdld", 2 * eb)], thh])
                    for tc in range(NTCS):
                        io.waits("tensor", [toks.get(("pdev", tc))])
                        for fc in range(NH1):
                            mm = t.matmul(pds[tc][:],
                                          hh[:, fc * FT + 128 * tc:fc * FT + 128 * (tc + 1)],
                                          wdh[0][:, 512 * fc:512 * (fc + 1)],
                                          start=(fc == 0), stop=False)
                    toks[("wdslot", 0)] = io.mark_mm(mm)
                    if eb + 1 < 4:
                        issue_wd(2 * eb + 2)
                    io.waits("tensor", [toks[("wdld", 2 * eb + 1)]])
                    for tc in range(NTCS):
                        i = tc % 2
                        trow = t0 + 128 * tc
                        io.waits("sync", [toks.get(("ptile", i)), toks.get(("xev", i)),
                                          toks.get(("partial_w", seq))])
                        tp_ = io.dma("sync", pts[i][:],
                                     ps[trow:trow + 128, 512 * eb:512 * (eb + 1)], ("pt", i))
                        tx_ = io.dma("sync", xes[i][:],
                                     xs[trow:trow + 128, 512 * eb:512 * (eb + 1)], ("xe", i))
                        for fc in range(NH1, NFC):
                            mm = t.matmul(pds[tc][:],
                                          hh[:, fc * FT + 128 * tc:fc * FT + 128 * (tc + 1)],
                                          wdh[1][:, 512 * (fc - NH1):512 * (fc - NH1) + 512],
                                          start=False, stop=(fc == NFC - 1))
                        tpe = io.mark_mm(mm)
                        io.waits("vector", [tpe, tp_, toks.get(("ot_po", i)),
                                            toks.get(("otile", i)), tsdb])
                        v.tensor_tensor(ots[i][:], pds[tc][:], sdb[:], ALU.mult)
                        v.drain()
                        v.tensor_tensor(ots[i][:], ots[i][:], pts[i][:], ALU.add)
                        tdv = io.mark("vector")
                        toks[("ptile", i)] = tdv
                        toks[("pdev", tc)] = tdv
                        io.waits("gpsimd", [tdv, tx_])
                        g.tensor_tensor(ots[i][:], ots[i][:], xes[i][:], ALU.add)
                        tpo = io.mark("gpsimd")
                        toks[("ot_po", i)] = tpo
                        toks[("xev", i)] = tpo
                        io.wait("sync", tpo)
                        toks[("otile", i)] = io.dma(
                            "sync", os_[trow:trow + 128, 512 * eb:512 * (eb + 1)], ots[i][:],
                            ("ot", i))
                    toks[("wdslot", 1)] = io.now("pe")
                    if eb + 1 < 4:
                        issue_wd(2 * eb + 3)
        io.barrier()


def build_program():
    nc = bass.Bass("TRN2", target_bir_lowering=False)

    x = nc.dram_tensor("x", [NSEQ, S, D], F32, kind="ExternalInput")
    out = nc.dram_tensor("out", [NSEQ, S, D], F32, kind="ExternalOutput")
    wqT = nc.dram_tensor("wqT", [D, D], F32, kind="ExternalInput")
    wkT = nc.dram_tensor("wkT", [D, D], F32, kind="ExternalInput")
    wvT = nc.dram_tensor("wvT", [D, D], F32, kind="ExternalInput")
    woT = nc.dram_tensor("woT", [D, D], F32, kind="ExternalInput")
    U8 = mybir.dt.uint8
    wgT = nc.dram_tensor("wgT", [D, FF], U8, kind="ExternalInput")
    wuT = nc.dram_tensor("wuT", [D, FF], U8, kind="ExternalInput")
    wdT = nc.dram_tensor("wdT", [FF, D], U8, kind="ExternalInput")
    memT = nc.dram_tensor("memT", [H, HD, M, HD], F32, kind="ExternalInput")
    mnT = nc.dram_tensor("mnT", [H, HD, M], F32, kind="ExternalInput")
    gatei = nc.dram_tensor("gatei", [1, H], F32, kind="ExternalInput")
    sgi = nc.dram_tensor("sgi", [128, NFC], F32, kind="ExternalInput")
    sui = nc.dram_tensor("sui", [128, NFC], F32, kind="ExternalInput")
    sdi = nc.dram_tensor("sdi", [1, D], F32, kind="ExternalInput")
    maski = nc.dram_tensor("maski", [128, 128], F32, kind="ExternalInput")
    identi = nc.dram_tensor("identi", [128, 128], BF16, kind="ExternalInput")

    wgB = nc.dram_tensor("wgB", [D, FF], BF16, kind="Internal")
    wuB = nc.dram_tensor("wuB", [D, FF], BF16, kind="Internal")
    wdB = nc.dram_tensor("wdB", [FF, D], BF16, kind="Internal")
    partial = nc.dram_tensor("partial", [NSEQ, S, D], F32, kind="Internal")
    xnF = nc.dram_tensor("xnF", [NSEQ, D, S], BF16, kind="Internal")

    est = contextlib.ExitStack()
    dma_pool = [f"d{i:02d}" for i in range(32)]
    sem_names = ["pe", "dv", "ac", "po", "bar"] + dma_pool
    sems = {n: est.enter_context(nc.semaphore(n)) for n in sem_names}
    io = Ctx(nc, sems, dma_pool)
    t, v, s, g, y = nc.tensor, nc.vector, nc.scalar, nc.gpsimd, nc.sync
    toks = {}

    with est:
        ident = sb(est, nc, "ident", [128, 128], BF16)
        mask = sb(est, nc, "mask", [128, 128], F32)
        ones_bf = sb(est, nc, "ones_bf", [128, 1], BF16)
        onesr_f = sb(est, nc, "onesr_f", [1, 128], F32)
        ones1_f = sb(est, nc, "ones1_f", [1, 1], F32)
        onesc_f = sb(est, nc, "onesc_f", [128, 1], F32)
        epsb = sb(est, nc, "epsb", [128, 1], F32)
        gsb = sb(est, nc, "gsb", [1, H], F32)
        gsig = sb(est, nc, "gsig", [1, H], F32)
        g1b = sb(est, nc, "g1b", [128, H], F32)
        mnsb = sb(est, nc, "mnsb", [128, H * M], F32)
        mnb16 = sb(est, nc, "mnb16", [128, H * M], BF16)
        ne4 = sb(est, nc, "ne4", [1, M], F32)
        sgc = sb(est, nc, "sgc", [128, NFC], F32)
        suc = sb(est, nc, "suc", [128, NFC], F32)
        smp = sb(est, nc, "smp", [1, 128], F32)
        cvb_i = [sb(est, nc, f"cvb_i{i}", [128, 512], BF16) for i in range(2)]
        cvb_o = [sb(est, nc, f"cvb_o{i}", [128, 512], BF16) for i in range(2)]
        est.enter_context(nc.Block())
        if True:
            T = dict(xs=x.ap().rearrange("b s d -> (b s) d"),
                     os=out.ap().rearrange("b s d -> (b s) d"),
                     ps=partial.ap().rearrange("b s d -> (b s) d"),
                     wqT=wqT, wkT=wkT, wvT=wvT, woT=woT, wgB=wgB, wuB=wuB, wdB=wdB,
                     memT=memT, ident=ident, mask=mask, ones_bf=ones_bf,
                     onesr_f=onesr_f, ones1_f=ones1_f, epsb=epsb, gsig=gsig,
                     g1b=g1b, mnb16=mnb16, ne4=ne4, smp=smp, sgc=sgc, suc=suc,
                     xnF=xnF, tffnw=None, cv_step=lambda n=1: None)

            tld = [io.dma("sync", ident[:], identi.ap(), "consts"),
                   io.dma("sync", mask[:], maski.ap(), "consts"),
                   io.dma("sync", gsb[:], gatei.ap(), "consts"),
                   io.dma("sync", sgc[:], sgi.ap(), "consts"),
                   io.dma("sync", suc[:], sui.ap(), "consts"),
                   io.dma("sync", mnsb[:], mnT.ap().rearrange("h d m -> d h m"), "consts")]
            g.memset(ones_bf[:], 1.0)
            g.memset(onesr_f[:], 1.0)
            g.memset(ones1_f[:], 1.0)
            g.memset(onesc_f[:], 1.0)
            g.memset(epsb[:], LN_EPS)
            tpo = io.mark("gpsimd")
            io.waits("scalar", tld + [tpo])
            s.activation(gsig[:], gsb[:], AF.Sigmoid, bias=0.0, scale=1.0)
            tac = io.mark("scalar")
            io.waits("vector", [tac] + tld)
            v.tensor_scalar(smp[0:1, 0:H], gsig[:], -1.0, 1.0, ALU.mult, ALU.add)
            v.tensor_copy(mnb16[:], mnsb[:])
            tdv = io.mark("vector")

            esp = contextlib.ExitStack()
            with esp:
                pbc = pt(esp, nc, "pbc", [128, 512])
                io.wait("tensor", tdv)
                mm = t.matmul(pbc[:, 0:H], onesr_f[:], smp[0:1, 0:H], start=True, stop=True)
                tpe = io.mark_mm(mm)
                io.wait("vector", tpe)
                v.tensor_copy(g1b[:], pbc[:, 0:H])
                tg1b = io.mark("vector")
                io.waits("tensor", [tg1b])
                mm = t.matmul(pbc[0:1, 64:128], onesc_f[:], mnsb[:], start=True, stop=True)
                tpe2 = io.mark_mm(mm)
                io.wait("vector", tpe2)
                v.tensor_copy(smp[0:1, 16:80], pbc[0:1, 64:128])
                v.drain()
                v.tensor_reduce(ne4[:], bass.AP(ap_of(smp), 16, [[128, 1], [1, 4], [4, 16]]),
                                AX.X, ALU.add)
                v.drain()
                v.tensor_single_scalar(ne4[:], ne4[:], EPS, ALU.is_ge)
                tne = io.mark("vector")

            io.barrier()

            cvq = []
            if DO_FFN:
                tiles = []
                for mat, dst in ((wgT, wgB), (wuT, wuB)):
                    for rt in range(16):
                        for st_ in range(11):
                            tiles.append((mat, dst, rt, 512 * st_))
                for rt in range(NFC):
                    for st_ in range(4):
                        tiles.append((wdT, wdB, rt, 512 * st_))
                cvst = {"load": 0, "proc": 0, "tin": {}}

                def cv_load(i):
                    mat, dst, rt, c0 = tiles[i]
                    par = i % 2
                    io.waits("gpsimd", [toks.get(("cvi_war", par))])
                    cvst["tin"][i] = io.dma(
                        "gpsimd", cvb_i[par][:],
                        mat.ap()[128 * rt:128 * (rt + 1), c0:c0 + 512], ("cvi", par))

                def cv_proc(i):
                    mat, dst, rt, c0 = tiles[i]
                    par = i % 2
                    io.waits("scalar", [cvst["tin"].pop(i), toks.get(("cvo_w", par))])
                    s.activation(cvb_o[par][:], cvb_i[par][:], AF.Copy,
                                 bias=-128.0, scale=1.0)
                    tcv = io.mark("scalar")
                    toks[("cvi_war", par)] = tcv
                    io.wait("sync", tcv)
                    toks[("cvo_w", par)] = io.dma(
                        "sync", dst.ap()[128 * rt:128 * (rt + 1), c0:c0 + 512],
                        cvb_o[par][:], ("cvo", par))

                def cv_tick():
                    while cvst["load"] < min(cvst["proc"] + 2, len(tiles)):
                        cv_load(cvst["load"])
                        cvst["load"] += 1
                    if cvst["proc"] < cvst["load"]:
                        cv_proc(cvst["proc"])
                        cvst["proc"] += 1
                        return True
                    return False

                for _ in tiles:
                    cvq.append(cv_tick)

            def cv_step(n=1):
                for _ in range(n):
                    if cvq:
                        cvq.pop(0)()
            T["cv_step"] = cv_step
            T["cvq"] = cvq
            T["sdi"] = sdi

            for seq in range(NSEQ):
                if DO_ATTN:
                    emit_attn_seq(io, toks, seq, T)
                while cvq:
                    cv_step()
                if T.get("tffnw") is None:
                    T["tffnw"] = [io.now_ch(("cvo", 0)), io.now_ch(("cvo", 1)),
                                  io.now_ch(("cvi", 0)), io.now_ch(("cvi", 1))]
                if DO_FFN:
                    emit_ffn_seq(io, toks, seq, T)

            io.barrier()
            for ch, q in io.chan.items():
                if io.ct[q] > 0:
                    y.wait_ge(sems[q], io.ct[q])

    nc.finalize()
    return nc


_nc_cache = {}


def _get_program():
    key = (DO_ATTN, DO_FFN)
    if key not in _nc_cache:
        _nc_cache[key] = build_program()
    return _nc_cache[key]


def kernel(L0_attn_hidden_states, L0_attn_memories, L0_attn_memory_norms,
           L0_attn_w_q, L0_attn_w_k, L0_attn_w_v, L0_attn_w_o, L0_attn_gate,
           L0_ffn_wq_gate, L0_ffn_s_gate, L0_ffn_wq_up, L0_ffn_s_up,
           L0_ffn_wq_down, L0_ffn_s_down):
    x = np.asarray(L0_attn_hidden_states, np.float32)
    co = np.ascontiguousarray
    shared = {
        "wqT": co(np.asarray(L0_attn_w_q, np.float32).T),
        "wkT": co(np.asarray(L0_attn_w_k, np.float32).T),
        "wvT": co(np.asarray(L0_attn_w_v, np.float32).T),
        "woT": co(np.asarray(L0_attn_w_o, np.float32).T),
        "wgT": co(np.asarray(L0_ffn_wq_gate, np.int32).T.astype(np.uint8)),
        "wuT": co(np.asarray(L0_ffn_wq_up, np.int32).T.astype(np.uint8)),
        "wdT": co(np.asarray(L0_ffn_wq_down, np.int32).T.astype(np.uint8)),
        "memT": co(np.transpose(np.asarray(L0_attn_memories, np.float32), (1, 2, 0, 3))),
        "mnT": co(np.transpose(np.asarray(L0_attn_memory_norms, np.float32), (1, 2, 0))),
        "gatei": co(np.asarray(L0_attn_gate, np.float32).reshape(1, H)),
        "sgi": co(np.asarray(L0_ffn_s_gate, np.float32).reshape(NFC, 128).T),
        "sui": co(np.asarray(L0_ffn_s_up, np.float32).reshape(NFC, 128).T),
        "sdi": co(np.asarray(L0_ffn_s_down, np.float32).reshape(1, D)),
        "maski": co(np.triu(np.ones((128, 128), np.float32))),
        "identi": np.eye(128, dtype=BFD),
    }
    in_maps = []
    for c in range(8):
        m = dict(shared)
        m["x"] = co(x[NSEQ * c:NSEQ * (c + 1)])
        in_maps.append(m)
    nc = _get_program()
    res = run_bass_kernel_spmd(nc, in_maps, core_ids=list(range(8)))
    return np.concatenate([r["out"] for r in res.results], axis=0)


if __name__ == "__main__":
    nc = _get_program()
    print("instructions:", len(list(nc.all_instructions())))


# version 23
# speedup vs baseline: 1.0624x; 1.0624x over previous
import numpy as np
import ml_dtypes
import contextlib

import concourse.bass as bass
import concourse.mybir as mybir
from concourse.bass_utils import run_bass_kernel_spmd

F32 = mybir.dt.float32
BF16 = mybir.dt.bfloat16
I32 = mybir.dt.int32
AF = mybir.ActivationFunctionType
ALU = mybir.AluOpType
AX = mybir.AxisListType

BFD = ml_dtypes.bfloat16

D = 2048
H = 16
HD = 128
M = 4
FF = 5632
S = 2048
NSEQ = 2
NKC = D // 128
NTC = S // 128
NFC = FF // 128
FT = 1024
NFST = S // FT
SCALE = float(HD) ** -0.5
EPS = 1e-6
LN_EPS = 1e-5

DO_ATTN = True
DO_FFN = True


def ap_of(buf):
    return buf.handle if hasattr(buf, "handle") else buf


_uid = [0]

def sb(es, nc, name, shape, dtype):
    _uid[0] += 1
    return es.enter_context(nc.sbuf_tensor(f"{name}_{_uid[0]}", shape, dtype))

def pt(es, nc, name, shape, dtype=F32):
    _uid[0] += 1
    return es.enter_context(nc.psum_tensor(f"{name}_{_uid[0]}", shape, dtype))


class Ctx:

    def __init__(self, nc, sems, dma_pool):
        self.nc = nc
        self.sems = sems
        self.ct = {k: 0 for k in sems}
        self.dma_pool = list(dma_pool)
        self.chan = {}
        self.settled = {}

    def _ch(self, channel):
        if channel not in self.chan:
            if not self.dma_pool:
                raise RuntimeError("out of DMA semaphores")
            self.chan[channel] = self.dma_pool.pop(0)
        return self.chan[channel]

    def dma(self, eng, dst, src, channel, burst=False):
        q = self._ch(channel)
        e = getattr(self.nc, eng)
        if not burst and self.ct[q] > self.settled.get(q, 0):
            e.wait_ge(self.sems[q], self.ct[q])
            self.settled[q] = self.ct[q]
        e.dma_start(dst, src).then_inc(self.sems[q], 16)
        self.ct[q] += 16
        return (q, self.ct[q])

    def mark(self, eng):
        n = {"vector": "dv", "scalar": "ac", "gpsimd": "po"}[eng]
        getattr(self.nc, eng).drain().then_inc(self.sems[n], 1)
        self.ct[n] += 1
        return (n, self.ct[n])

    def mark_mm(self, instr):
        instr.then_inc(self.sems["pe"], 1)
        self.ct["pe"] += 1
        return ("pe", self.ct["pe"])

    def waits(self, eng, toks):
        best = {}
        flat = []
        for t in toks:
            if t is None:
                continue
            if isinstance(t, list):
                flat.extend(x for x in t if x is not None)
            else:
                flat.append(t)
        for n, v in flat:
            best[n] = max(best.get(n, 0), v)
        for n, v in best.items():
            getattr(self.nc, eng).wait_ge(self.sems[n], v)

    def wait(self, eng, tok):
        self.waits(eng, [tok])

    def now(self, name):
        return (name, self.ct[name])

    def now_ch(self, channel):
        if channel not in self.chan:
            return None
        q = self.chan[channel]
        return (q, self.ct[q])

    def barrier(self):
        for ch, q in self.chan.items():
            if self.ct[q] > self.settled.get(q, 0):
                self.nc.sync.wait_ge(self.sems[q], self.ct[q])
                self.settled[q] = self.ct[q]
        bar = self.sems["bar"]
        base = self.ct["bar"]
        for eng in ("vector", "scalar", "gpsimd"):
            getattr(self.nc, eng).drain().then_inc(bar, 1)
        self.nc.tensor.nop().then_inc(bar, 1)
        self.nc.sync.nop().then_inc(bar, 1)
        self.ct["bar"] += 5
        for eng in ("vector", "scalar", "gpsimd", "tensor", "sync"):
            getattr(self.nc, eng).wait_ge(bar, base + 5)


def emit_ln_chunk(c, nchunks, io, bufs, toks, xsrc, col_of_kc, stage_w, sxn_psums):
    nc = io.nc
    v, s, t, g = nc.vector, nc.scalar, nc.tensor, nc.gpsimd
    xt = bufs["xtile"][c % 2]
    jA, jB, st1 = bufs["junkA"], bufs["junkB"], bufs["stat1"]
    xnt = bufs["xn_tm"][c % 2]
    tp = bufs["tpsum"]

    io.waits("sync", [toks.get(("xtile", c % 2))])
    ld = io.dma("sync", xt[:], xsrc, ("xt", c % 2))
    o = 8 * (c % 2)
    io.waits("scalar", [ld, toks.get(("st1war", c % 2))])
    s.activation(jA[:], xt[:], AF.Square, bias=0.0, scale=1.0,
                 accum_out=st1[:, o + 1:o + 2])
    s.activation(jB[:, 1024:2048], xt[:, 0:1024], AF.Identity, bias=0.0, scale=1.0,
                 accum_out=st1[:, o + 7:o + 8])
    tac = io.mark("scalar")
    io.wait("vector", ld)
    v.tensor_scalar(jB[:, 0:1024], xt[:, 1024:2048], 0.0, 0.0, ALU.add, ALU.add,
                    accum_out=st1[:, o:o + 1])
    io.wait("vector", tac)
    v.drain()
    v.tensor_tensor(st1[:, o + 6:o + 7], st1[:, o:o + 1], st1[:, o + 7:o + 8], ALU.add)
    v.drain()
    v.tensor_scalar_mul(st1[:, o + 2:o + 3], st1[:, o + 6:o + 7], 1.0 / 2048.0)
    v.drain()
    v.tensor_mul(st1[:, o + 3:o + 4], st1[:, o + 2:o + 3], st1[:, o + 2:o + 3])
    io.wait("vector", tac)
    v.drain()
    v.scalar_tensor_tensor(st1[:, o + 4:o + 5], st1[:, o + 1:o + 2], 1.0 / 2048.0,
                           st1[:, o + 3:o + 4], ALU.mult, ALU.subtract)
    tvar = io.mark("vector")
    toks[("st1war", c % 2)] = tvar
    io.wait("scalar", tvar)
    s.activation(st1[:, o + 6:o + 7], st1[:, o + 4:o + 5], AF.Ln,
                 bias=bufs["epsb"][:], scale=1.0)
    s.drain()
    s.activation(st1[:, o + 5:o + 6], st1[:, o + 6:o + 7], AF.Exp,
                 bias=0.0, scale=-0.5)
    trstd = io.mark("scalar")
    io.waits("vector", [trstd, toks.get(("xn_tm", c % 2))])
    v.tensor_scalar(xnt[:], xt[:], st1[:, o + 2:o + 3], st1[:, o + 5:o + 6],
                    ALU.subtract, ALU.mult)
    tnorm = io.mark("vector")
    toks[("xtile", c % 2)] = tnorm
    for grp in range(4):
        pg = tp[grp % 2]
        io.waits("tensor", [tnorm, toks.get(("tpsum", grp % 2))])
        for j in range(4):
            kc = grp * 4 + j
            mm = t.transpose(pg[:, 128 * j:128 * (j + 1)],
                             xnt[:, 128 * kc:128 * (kc + 1)], bufs["ident"][:])
        tpe = io.mark_mm(mm)
        eng = "vector" if grp % 2 == 0 else "scalar"
        io.wait(eng, tpe)
        dst = bass.AP(ap_of(bufs["xn"]), col_of_kc(grp * 4),
                      [[bufs["xn_psz"], 128], [stage_w, 4], [1, 128]])
        if eng == "vector":
            v.tensor_copy(dst, pg[:])
        else:
            s.copy(dst, pg[:])
        toks[("tpsum", grp % 2)] = io.mark(eng)
    if sxn_psums is not None:
        for q4 in range(4):
            mm = t.matmul(sxn_psums[q4][:], bufs["ones_bf"][:],
                          xnt[:, 512 * q4:512 * (q4 + 1)],
                          start=(c == 0), stop=(c == nchunks - 1))
        if c == nchunks - 1:
            toks["sxn"] = io.mark_mm(mm)
        toks[("xn_tm", c % 2)] = ("pe", io.ct["pe"] + (0 if c == nchunks - 1 else 1))
        if c < nchunks - 1:
            io.mark_mm(mm)
    else:
        toks[("xn_tm", c % 2)] = toks[("tpsum", 1)]


def emit_attn_seq(io, toks, seq, T):
    nc = io.nc
    t, v, s, g, y = nc.tensor, nc.vector, nc.scalar, nc.gpsimd, nc.sync
    xs, ps, os_ = T["xs"], T["ps"], T["os"]
    r0 = seq * S

    es0 = contextlib.ExitStack()
    with es0:
        xn = sb(es0, nc, "xn_a", [128, NKC * S], BF16)
        mix = sb(es0, nc, "mix", [128, NTC * D], BF16)
        xbarT = sb(es0, nc, "xbarT", [128, NKC], BF16)
        wppb = sb(es0, nc, "wppb", [128, M], F32)
        es1 = contextlib.ExitStack()
        with es1:
            xtA = sb(es1, nc, "xtileA", [128, D], F32)
            xtB = sb(es1, nc, "xtileB", [128, D], F32)
            jA = sb(es1, nc, "junkA", [128, D], F32)
            jB = sb(es1, nc, "junkB", [128, D], F32)
            st1 = sb(es1, nc, "stat1", [128, 16], F32)
            xnA = sb(es1, nc, "xn_tmA", [128, D], BF16)
            xnB = sb(es1, nc, "xn_tmB", [128, D], BF16)
            sxn_sb = sb(es1, nc, "sxn_sb", [1, D], F32)
            tpA = pt(es1, nc, "tpA", [128, 512], BF16)
            tpB = pt(es1, nc, "tpB", [128, 512], BF16)
            sx0 = pt(es1, nc, "sx0", [1, 512])
            sx1 = pt(es1, nc, "sx1", [1, 512])
            sx2 = pt(es1, nc, "sx2", [1, 512])
            sx3 = pt(es1, nc, "sx3", [1, 512])
            pxb = pt(es1, nc, "pxb", [128, 16])
            bufs = dict(T)
            bufs.update(xn=xn, xn_psz=NKC * S, xtile=[xtA, xtB], junkA=jA, junkB=jB,
                        stat1=st1, xn_tm=[xnA, xnB], tpsum=[tpA, tpB])
            sxp = [sx0, sx1, sx2, sx3]
            for c in range(NTC):
                emit_ln_chunk(c, NTC, io, bufs, toks, xs[r0 + 128 * c:r0 + 128 * (c + 1), :],
                              lambda kc, c=c: kc * S + 128 * c, S, sxp)
                T["cv_step"](1)
            io.wait("vector", toks["sxn"])
            for q4 in range(4):
                v.tensor_copy(sxn_sb[:, 512 * q4:512 * (q4 + 1)], sxp[q4][:])
            tsxe = io.mark("vector")
            io.wait("tensor", tsxe)
            for kc in range(NKC):
                mm = t.matmul(pxb[:, kc:kc + 1], sxn_sb[0:1, 128 * kc:128 * (kc + 1)],
                              T["ones1_f"][:], start=True, stop=True)
            tpe = io.mark_mm(mm)
            io.wait("vector", tpe)
            v.tensor_scalar_mul(xbarT[:], pxb[:], 1.0 / 2048.0)
            toks["xbarT"] = io.mark("vector")
            io.waits("sync", [io.now("dv"), io.now("ac")])
            toks[("xnf", seq)] = io.dma(
                "sync", bass.AP(T["xnF"], seq * D * S, [[S, 128], [128 * S, NKC], [1, S]]),
                xn[:].rearrange("p (c t) -> p c t", c=NKC), "xnf")
        io.barrier()

        es2 = contextlib.ExitStack()
        with es2:
            wq0 = sb(es2, nc, "wq0", [128, NKC * 128], BF16)
            wq1 = sb(es2, nc, "wq1", [128, NKC * 128], BF16)
            wk0 = sb(es2, nc, "wk0", [128, NKC * 128], BF16)
            wv0 = sb(es2, nc, "wv0", [128, NKC * 128], BF16)
            mem0 = sb(es2, nc, "mem0", [128, M * 128], BF16)
            mem1 = sb(es2, nc, "mem1", [128, M * 128], BF16)
            mem2 = sb(es2, nc, "mem2", [128, M * 128], BF16)
            sqb_ = [sb(es2, nc, f"sq{i}", [128, S], BF16) for i in range(2)]
            skb_ = [sb(es2, nc, f"sk{i}", [128, S], BF16) for i in range(2)]
            skt0 = sb(es2, nc, "sktm0", [128, S], BF16)
            sktb = [skt0, skt0]
            vtmb = [sb(es2, nc, f"vtm{i}", [128, S], BF16) for i in range(2)]
            Sball = [sb(es2, nc, f"Sball{i}", [128, NTC * 133], BF16) for i in range(2)]
            wppbb = [sb(es2, nc, f"wppb{i}", [128, 8], F32) for i in range(2)]
            vst0 = sb(es2, nc, "vst0", [128, 512], BF16)
            vst1 = sb(es2, nc, "vst1", [128, 512], BF16)
            vstb = [vst0, vst1]
            el0 = sb(es2, nc, "el0", [128, 512], BF16)
            el1 = sb(es2, nc, "el1", [128, 512], BF16)
            el2 = sb(es2, nc, "el2", [128, 512], BF16)
            el3 = sb(es2, nc, "el3", [128, 512], BF16)
            msc0 = sb(es2, nc, "msc0", [128, 128], BF16)
            msc1 = sb(es2, nc, "msc1", [128, 128], BF16)
            mm640a = sb(es2, nc, "mm640a", [128, 640], F32)
            mm640b = sb(es2, nc, "mm640b", [128, 640], F32)
            den5 = sb(es2, nc, "den5", [128, 8], F32)
            betab = sb(es2, nc, "betab", [128, 8], F32)
            qbarb = sb(es2, nc, "qbarb", [128, 1], BF16)
            pA = pt(es2, nc, "pA", [128, 1024])
            pKV = pt(es2, nc, "pKV", [128, 129])
            pU = pt(es2, nc, "pU", [128, 512])
            pDNa = pt(es2, nc, "pDNa", [128, 136])
            pDNb = pt(es2, nc, "pDNb", [128, 136])
            pSC2 = pt(es2, nc, "pSC2", [128, 256])
            pD = pt(es2, nc, "pD", [128, 512], BF16)
            wqb, wkb, wvb = [wq0, wq1], [wk0, wk0], [wv0, wv0]
            memb = [mem0, mem1, mem2]
            mscb = [msc0, msc1]
            m6b = [mm640a, mm640b]
            rl = T["smp"]

            def issue_head_weights(h):
                p, p3 = h % 2, h % 3
                wt_toks = []
                io.waits("gpsimd", [toks.get(("wslab", p)), toks.get(("memslab", p3))])
                for wi, (wt, bb, ch) in enumerate(((T["wqT"], wqb[p], ("wh", p)),
                                                   (T["wkT"], wkb[p], "whk"),
                                                   (T["wvT"], wvb[p], "whv"))):
                    src = bass.AP(wt, 128 * h, [[D, 128], [128 * D, NKC], [1, 128]])
                    dst = bb[:].rearrange("p (c e) -> p c e", c=NKC)
                    wt_toks.append(io.dma("gpsimd", dst, src, ch))
                wt_toks.append(io.dma("gpsimd",
                                      memb[p3][:].rearrange("p (m e) -> p m e", m=M),
                                      T["memT"].ap()[h], ("mem", p3)))
                return wt_toks

            def sigma_evac(blk, dst):
                tb = blk % 2
                ea = el0 if tb == 0 else el1
                eb = el2 if tb == 0 else el3
                col = 512 * blk
                tpe = toks[("projmm", tb)]
                io.waits("scalar", [tpe, toks.get(("el", tb))])
                s.activation(ea[:], pA[:, 512 * tb:512 * (tb + 1)], AF.Exp,
                             bias=0.0, scale=1.0)
                s.activation(eb[:], pA[:, 512 * tb:512 * (tb + 1)], AF.Relu,
                             bias=0.0, scale=1.0)
                tac = io.mark("scalar")
                io.wait("vector", tac)
                v.scalar_tensor_tensor(dst[:, col:col + 512], ea[:], 1.0,
                                       eb[:], ALU.min, ALU.add)
                tdv = io.mark("vector")
                toks[("pA", tb)] = tac
                toks[("el", tb)] = tdv
                return tdv

            pbq = []

            def pb_step(n=1):
                T["cv_step"](2)
                for _ in range(n):
                    if pbq:
                        pbq.pop(0)()

            pSCb_ = [pSC2[:, 0:128], pSC2[:, 128:256]]
            pDNp = [pDNa, pDNb]

            def make_phaseB(hh, twh):
                ph, p3 = hh % 2, hh % 3
                sq, sk = sqb_[ph], skb_[ph]
                v_tm = vtmb[ph]
                Sb_all = Sball[ph]
                wppb = wppbb[ph]

                def first():
                    io.waits("tensor", [toks[("sq", ph)], toks[("sk", ph)],
                                        toks.get(("pSC_free", 0))])
                    mm = t.matmul(pSCb_[0], sk[:, 0:128], sq[:, 0:128],
                                  start=True, stop=True)
                    toks[("t_sc", 0)] = io.mark_mm(mm)
                    io.waits("vector", [toks[("t_sc", 0)], toks.get(("msc_war", 0))])
                    v.tensor_tensor(mscb[0][:], pSCb_[0], T["mask"][:], ALU.mult)
                    tA = io.mark("vector")
                    toks[("msk", 0)] = tA
                    toks[("pSC_free", 0)] = tA

                def chunk(c):
                    col = 128 * c
                    mp = mscb[c % 2]
                    m6 = m6b[c % 2]
                    pDN = pDNp[c % 2]
                    io.waits("tensor", [toks[("msk", c)], toks.get(("pDN_free", c % 2)),
                                        toks[("Sslot", ph, c)]])
                    t.matmul(pDN[:, 0:5], sq[:, col:col + 128],
                             Sb_all[:, 133 * c + 128:133 * c + 133], start=True, stop=False)
                    t.matmul(pDN[:, 0:1], mp[:], T["ones_bf"][:], start=False, stop=False)
                    t.matmul(pDN[:, 8:136], sq[:, col:col + 128],
                             Sb_all[:, 133 * c:133 * c + 128], start=False, stop=False)
                    mm = t.matmul(pDN[:, 8:136], mp[:], v_tm[:, col:col + 128],
                                  start=False, stop=True)
                    t_mms = io.mark_mm(mm)
                    toks[("msc_war", c % 2)] = t_mms
                    if c + 1 < NTC:
                        io.waits("tensor", [toks.get(("pSC_free", (c + 1) % 2))])
                        mm = t.matmul(pSCb_[(c + 1) % 2], sk[:, col + 128:col + 256],
                                      sq[:, col + 128:col + 256], start=True, stop=True)
                        toks[("t_sc", c + 1)] = io.mark_mm(mm)
                        mpn = mscb[(c + 1) % 2]
                        io.waits("vector", [toks[("t_sc", c + 1)],
                                            toks.get(("msc_war", (c + 1) % 2))])
                        v.tensor_tensor(mpn[:], pSCb_[(c + 1) % 2], T["mask"][:], ALU.mult)
                        tA = io.mark("vector")
                        toks[("msk", c + 1)] = tA
                        toks[("pSC_free", (c + 1) % 2)] = tA
                    io.waits("tensor", [twh[3], toks.get("pU_free")])
                    mm = t.matmul(pU[:], sq[:, col:col + 128], memb[p3][:],
                                  start=True, stop=True)
                    t_u = io.mark_mm(mm)
                    if c + 1 == NTC:
                        toks[("memslab", p3)] = t_u
                    io.waits("vector", [t_mms, toks.get(("m6_war", c % 2))])
                    v.tensor_scalar_max(den5[:, 0:5], pDN[:, 0:5], EPS)
                    v.drain()
                    v.reciprocal(den5[:, 0:5], den5[:, 0:5])
                    v.drain()
                    v.tensor_mul(betab[:, 0:5], wppb[:, 0:5], den5[:, 0:5])
                    v.drain()
                    io.wait("vector", t_u)
                    bcast = bass.AP(ap_of(betab), 1, [[8, 128], [1, 4], [0, 128]])
                    v.tensor_tensor(m6[:, 0:512], pU[:], bcast, ALU.mult)
                    tB0 = io.mark("vector")
                    toks["pU_free"] = tB0
                    v.tensor_scalar_mul(m6[:, 512:640], pDN[:, 8:136], betab[:, 0:1])
                    tn = io.mark("vector")
                    toks[("pDN_free", c % 2)] = tn
                    with nc.allow_low_precision(reason="fp32-accumulated reduce, bf16 store"):
                        v.tensor_reduce(mix[:, c * D + 128 * hh:c * D + 128 * (hh + 1)],
                                        bass.AP(ap_of(m6), 0, [[640, 128], [1, 128], [128, 5]]),
                                        AX.X, ALU.add)
                    tB = io.mark("vector")
                    toks[("m6_war", c % 2)] = tB
                    toks["mix_done"] = tB

                out = [first]
                for c in range(NTC):
                    out.append(lambda c=c: chunk(c))
                return out

            def emit_wpp(h):
                p = h % 2
                io.waits("tensor", tw[h] + [toks.get("xbarT"), toks.get("pKV_free")])
                for kc in range(NKC):
                    mm = t.matmul(pKV[:, 0:1], wqb[p][:, 128 * kc:128 * (kc + 1)],
                                  xbarT[:, kc:kc + 1], start=(kc == 0), stop=(kc == NKC - 1))
                tpe = io.mark_mm(mm)
                io.wait("scalar", tpe)
                s.activation(qbarb[:], pKV[:, 0:1], AF.Copy, bias=0.0, scale=1.0)
                tqb = io.mark("scalar")
                pb_step()
                io.waits("tensor", [tqb])
                mm = t.matmul(pKV[0:1, 4:8], qbarb[:], T["mnb16"][:, M * h:M * (h + 1)],
                              start=True, stop=True)
                tpe = io.mark_mm(mm)
                io.wait("vector", tpe)
                v.tensor_scalar_mul(rl[0:1, 80:84], pKV[0:1, 4:8], SCALE)
                v.drain()
                v.tensor_reduce(rl[0:1, 84:85], rl[0:1, 80:84], AX.X, ALU.max)
                v.drain()
                v.tensor_scalar_mul(rl[0:1, 85:86], rl[0:1, 84:85], -1.0)
                tdv = io.mark("vector")
                io.wait("scalar", tdv)
                s.activation(rl[0:1, 88:92], rl[0:1, 80:84], AF.Exp,
                             bias=rl[0:1, 85:86], scale=1.0)
                tac = io.mark("scalar")
                io.wait("vector", tac)
                v.tensor_mul(rl[0:1, 88:92], rl[0:1, 88:92], T["ne4"][:])
                v.drain()
                v.tensor_reduce(rl[0:1, 92:93], rl[0:1, 88:92], AX.X, ALU.add)
                v.drain()
                v.tensor_scalar_max(rl[0:1, 93:94], rl[0:1, 92:93], 1e-30)
                v.drain()
                v.reciprocal(rl[0:1, 94:95], rl[0:1, 93:94])
                v.drain()
                v.tensor_scalar(rl[0:1, 96:100], rl[0:1, 88:92], rl[0:1, 94:95],
                                T["gsig"][0:1, h:h + 1], ALU.mult, ALU.mult)
                v.tensor_copy(rl[0:1, 95:96], rl[0:1, h:h + 1])
                tdv = io.mark("vector")
                pb_step()
                io.waits("tensor", [tdv])
                mm = t.matmul(pKV[:, 8:13], T["onesr_f"][:], rl[0:1, 95:100],
                              start=True, stop=True)
                tpe = io.mark_mm(mm)
                io.wait("vector", tpe)
                v.tensor_copy(wppbb[p][:, 0:5], pKV[:, 8:13])
                toks["pKV_free"] = io.mark("vector")

            tw = {0: issue_head_weights(0)}
            emit_wpp(0)
            for h in range(H):
                p, p3 = h % 2, h % 3
                sq, sk = sqb_[p], skb_[p]
                sk_tm, v_tm = sktb[p], vtmb[p]
                Sb_all = Sball[p]

                for blk in range(4):
                    tb = blk % 2
                    io.waits("tensor", [toks.get(("pA", tb))])
                    for kc in range(NKC):
                        mm = t.matmul(pA[:, 512 * tb:512 * (tb + 1)],
                                      wkb[p][:, 128 * kc:128 * (kc + 1)],
                                      xn[:, kc * S + 512 * blk:kc * S + 512 * (blk + 1)],
                                      start=(kc == 0), stop=(kc == NKC - 1))
                    toks[("projmm", tb)] = io.mark_mm(mm)
                    pb_step()
                    tdv = sigma_evac(blk, sk)
                toks[("sk", p)] = tdv
                for blk in range(4):
                    tb = blk % 2
                    vs = vstb[tb]
                    io.waits("tensor", [toks.get(("pA", tb))])
                    for kc in range(NKC):
                        mm = t.matmul(pA[:, 512 * tb:512 * (tb + 1)],
                                      wvb[p][:, 128 * kc:128 * (kc + 1)],
                                      xn[:, kc * S + 512 * blk:kc * S + 512 * (blk + 1)],
                                      start=(kc == 0), stop=(kc == NKC - 1))
                    tpe = io.mark_mm(mm)
                    pb_step()
                    io.waits("scalar", [tpe, toks.get(("vst", tb))])
                    s.copy(vs[:], pA[:, 512 * tb:512 * (tb + 1)])
                    tac = io.mark("scalar")
                    toks[("pA", tb)] = tac
                    io.waits("tensor", [tac, toks.get("pD_ev")])
                    for j in range(4):
                        mm = t.transpose(pD[:, 128 * j:128 * (j + 1)],
                                         vs[:, 128 * j:128 * (j + 1)], T["ident"][:])
                    tpe = io.mark_mm(mm)
                    toks[("vst", tb)] = tpe
                    io.wait("vector", tpe)
                    v.tensor_copy(v_tm[:, 512 * blk:512 * (blk + 1)], pD[:])
                    toks["pD_ev"] = io.mark("vector")
                for q4 in range(4):
                    io.waits("tensor", [toks[("sk", p)], toks.get("pD_ev")])
                    for j in range(4):
                        c = 4 * q4 + j
                        mm = t.transpose(pD[:, 128 * j:128 * (j + 1)],
                                         sk[:, 128 * c:128 * (c + 1)], T["ident"][:])
                    tpe = io.mark_mm(mm)
                    if q4 % 2 == 1:
                        pb_step()
                    eng = "vector" if q4 % 2 == 0 else "scalar"
                    io.wait(eng, tpe)
                    if q4 % 2 == 0:
                        v.tensor_copy(sk_tm[:, 512 * q4:512 * (q4 + 1)], pD[:])
                    else:
                        s.copy(sk_tm[:, 512 * q4:512 * (q4 + 1)], pD[:])
                    toks["pD_ev"] = io.mark(eng)
                tvtm = toks["pD_ev"]

                io.waits("gpsimd", [io.now("pe"), io.now("dv"), io.now("ac")])
                g.memset(Sb_all[:, 0:129], 0.0)
                src_mn = bass.AP(ap_of(T["mnb16"]), 4 * h, [[64, 128], [0, NTC], [1, 4]])
                dst_mn = bass.AP(ap_of(Sb_all), 129, [[NTC * 133, 128], [133, NTC], [1, 4]])
                g.tensor_copy(dst_mn, src_mn)
                tSi = io.mark("gpsimd")
                toks[("Sslot", p, 0)] = tSi
                toks[("Scp", -1)] = tSi

                if h + 1 < H:
                    toks[("wslab", (h + 1) % 2)] = io.now("pe")
                    tw[h + 1] = issue_head_weights(h + 1)

                for blk in range(4):
                    tb = blk % 2
                    kv_cs = [c for c in range(4 * blk, 4 * blk + 4) if c < NTC - 1]
                    io.waits("tensor", [toks.get(("pA", tb))])
                    for kc in range(NKC):
                        mm = t.matmul(pA[:, 512 * tb:512 * (tb + 1)],
                                      wqb[p][:, 128 * kc:128 * (kc + 1)],
                                      xn[:, kc * S + 512 * blk:kc * S + 512 * (blk + 1)],
                                      start=(kc == 0), stop=(kc == NKC - 1))
                        if kc == NKC - 1:
                            toks[("projmm", tb)] = io.mark_mm(mm)
                        if kc % 4 == 3 and kv_cs:
                            c = kv_cs.pop(0)
                            col = 128 * c
                            io.waits("tensor", [toks[("Scp", c - 1)]])
                            t.matmul(pKV[:, 0:128], sk_tm[:, col:col + 128],
                                     v_tm[:, col:col + 128], start=(c == 0), stop=False)
                            mm2 = t.matmul(pKV[:, 128:129], sk_tm[:, col:col + 128],
                                           T["ones_bf"][:], start=False,
                                           stop=(c == NTC - 2))
                            tkv = io.mark_mm(mm2)
                            io.waits("scalar", [tkv, toks.get(("Sslot", p, c - 1))])
                            s.copy(Sb_all[:, 133 * (c + 1):133 * (c + 1) + 129], pKV[:])
                            tcp = io.mark("scalar")
                            toks[("Scp", c)] = tcp
                            toks[("Sslot", p, c + 1)] = tcp
                    pb_step()
                    tdv = sigma_evac(blk, sq)
                toks[("sq", p)] = tdv
                toks["pKV_free"] = toks[("Scp", NTC - 2)]

                pb_step(NTC + 1)
                pbq = make_phaseB(h, tw[h])
                if h + 1 < H:
                    emit_wpp(h + 1)
            pb_step(NTC + 1)
        io.barrier()
        io.barrier()

        es3 = contextlib.ExitStack()
        with es3:
            woeb = [sb(es3, nc, f"woeb{i}", [128, NKC * 512], BF16) for i in range(2)]
            ot0 = sb(es3, nc, "otile0", [128, 512], F32)
            ot1 = sb(es3, nc, "otile1", [128, 512], F32)
            xe0 = sb(es3, nc, "xev0", [128, 512], F32)
            xe1 = sb(es3, nc, "xev1", [128, 512], F32)
            gA = pt(es3, nc, "gA", [128, 256], BF16)
            gB = pt(es3, nc, "gB", [128, 256], BF16)
            wo0 = pt(es3, nc, "wo0", [128, 512])
            wo1 = pt(es3, nc, "wo1", [128, 512])
            pairs = [(a, b) for a in range(NTC) for b in range(a, NKC)]
            gp = [gA, gB]
            for i, (a, b) in enumerate(pairs):
                pg = gp[i % 2]
                io.waits("tensor", [toks.get(("gT", i % 2)), toks.get("mix_done")])
                mm = t.transpose(pg[:, 0:128], mix[:, a * D + 128 * b:a * D + 128 * (b + 1)],
                                 T["ident"][:])
                if b != a:
                    mm = t.transpose(pg[:, 128:256],
                                     mix[:, b * D + 128 * a:b * D + 128 * (a + 1)],
                                     T["ident"][:])
                tpe = io.mark_mm(mm)
                io.wait("vector", tpe)
                v.tensor_copy(mix[:, b * D + 128 * a:b * D + 128 * (a + 1)], pg[:, 0:128])
                if b != a:
                    v.tensor_copy(mix[:, a * D + 128 * b:a * D + 128 * (b + 1)], pg[:, 128:256])
                toks[("gT", i % 2)] = io.mark("vector")
            tgt = toks[("gT", (len(pairs) - 1) % 2)]
            wop, oti, xei = [wo0, wo1], [ot0, ot1], [xe0, xe1]

            def issue_wo(eb):
                src = bass.AP(T["woT"], 512 * eb, [[D, 128], [128 * D, NKC], [1, 512]])
                io.waits("gpsimd", [toks.get(("woslot", eb % 2))])
                toks[("wold", eb)] = io.dma(
                    "gpsimd", woeb[eb % 2][:].rearrange("p (c e) -> p c e", c=NKC),
                    src, ("wo", eb % 2))

            issue_wo(0)
            issue_wo(1)
            for eb in range(4):
                for tc in range(NTC):
                    i = tc % 2
                    io.waits("tensor", [toks[("wold", eb)], tgt, toks.get(("wo_ev", i))])
                    for j in range(NKC):
                        mm = t.matmul(wop[i][:], mix[:, j * D + 128 * tc:j * D + 128 * (tc + 1)],
                                      woeb[eb % 2][:, 512 * j:512 * (j + 1)],
                                      start=(j == 0), stop=(j == NKC - 1))
                    tpe = io.mark_mm(mm)
                    if DO_FFN:
                        io.waits("scalar", [tpe, toks.get(("otile", i))])
                        s.activation(oti[i][:], wop[i][:], AF.Copy, bias=0.0, scale=1.0)
                        tev = io.mark("scalar")
                        dstt = ps
                    else:
                        io.waits("sync", [toks.get(("xev", i))])
                        tx = io.dma("sync", xei[i][:],
                                    xs[r0 + 128 * tc:r0 + 128 * (tc + 1), 512 * eb:512 * (eb + 1)],
                                    ("xe", i))
                        io.waits("vector", [tpe, tx, toks.get(("otile", i))])
                        v.tensor_tensor(oti[i][:], wop[i][:], xei[i][:], ALU.add)
                        tev = io.mark("vector")
                        toks[("xev", i)] = tev
                        dstt = os_
                    toks[("wo_ev", i)] = tev
                    io.wait("sync", tev)
                    toks[("otile", i)] = io.dma(
                        "sync", dstt[r0 + 128 * tc:r0 + 128 * (tc + 1), 512 * eb:512 * (eb + 1)],
                        oti[i][:], ("ot", i))
                toks[("woslot", eb % 2)] = io.now("pe")
                if eb + 2 < 4:
                    issue_wo(eb + 2)
            toks[("partial_w", seq)] = [io.now_ch(("ot", 0)), io.now_ch(("ot", 1))]
    io.barrier()


def emit_ffn_seq(io, toks, seq, T):
    nc = io.nc
    t, v, s, g, y = nc.tensor, nc.vector, nc.scalar, nc.gpsimd, nc.sync
    xs, ps, os_ = T["xs"], T["ps"], T["os"]
    NH1 = 22

    esq = contextlib.ExitStack()
    with esq:
        xn = sb(esq, nc, "xn_f", [128, NKC * FT], BF16)
        wgs = [sb(esq, nc, f"wgs{i}", [128, NKC * 128], BF16) for i in range(2)]
        wus = [sb(esq, nc, f"wus{i}", [128, NKC * 128], BF16) for i in range(2)]
        sdb = sb(esq, nc, "sdb", [128, D], F32)

        def issue_fc(fc):
            p = fc % 2
            io.waits("gpsimd", [toks.get(("wgu", p)), T["tffnw"]])
            a = io.dma("gpsimd", wgs[p][:].rearrange("p (c f) -> p c f", c=NKC),
                       bass.AP(T["wgB"], 128 * fc,
                               [[FF, 128], [128 * FF, NKC], [1, 128]]), ("wg", p))
            b = io.dma("gpsimd", wus[p][:].rearrange("p (c f) -> p c f", c=NKC),
                       bass.AP(T["wuB"], 128 * fc,
                               [[FF, 128], [128 * FF, NKC], [1, 128]]), ("wg", p),
                       burst=True)
            return [a, b]

        def load_xn(stg, war=None):
            xh = NKC // 2
            io.waits("sync", [toks[("xnf", seq)], toks.get(("xnst", 0)), war])
            r = [io.dma("sync", xn[:, 0:xh * FT].rearrange("p (c t) -> p c t", c=xh),
                        bass.AP(T["xnF"], seq * D * S + stg * FT,
                                [[S, 128], [128 * S, xh], [1, FT]]), ("xnst", 0))]
            io.waits("scalar", [toks[("xnf", seq)], toks.get(("xnst", 1)), war])
            r.append(io.dma("scalar",
                            xn[:, xh * FT:].rearrange("p (c t) -> p c t", c=xh),
                            bass.AP(T["xnF"], seq * D * S + xh * 128 * S + stg * FT,
                                    [[S, 128], [128 * S, xh], [1, FT]]), ("xnst", 1)))
            toks[("xnst", 0)] = r[0]
            toks[("xnst", 1)] = r[1]
            return r

        pre = {"t_xn": load_xn(0), "twg0": issue_fc(0)}
        for stg in range(NFST):
            t0 = seq * S + stg * FT
            esh = contextlib.ExitStack()
            emit_ffn_stage(io, toks, seq, T, stg, t0, esh, nc, xn, wgs, wus,
                           issue_fc, load_xn, pre, sdb)
    io.barrier()


def emit_ffn_stage(io, toks, seq, T, stg, t0, esh, nc, xn, wgs, wus,
                   issue_fc, load_xn, pre, sdb):
    t, v, s, g, y = nc.tensor, nc.vector, nc.scalar, nc.gpsimd, nc.sync
    xs, ps, os_ = T["xs"], T["ps"], T["os"]
    NH1 = 22
    if True:
        with esh:
            hh = sb(esh, nc, "hh", [128, NFC * FT], BF16)
            wdh = [sb(esh, nc, f"wdh{i}", [128, NH1 * 512], BF16) for i in range(2)]

            def issue_wd(g):
                eb, hf = g // 2, g % 2
                sl = g % 2
                io.waits("gpsimd", [toks.get(("wdslot", sl)), T["tffnw"]])
                toks[("wdld", g)] = io.dma(
                    "gpsimd", wdh[sl][:].rearrange("p (c e) -> p c e", c=NH1),
                    bass.AP(T["wdB"], 512 * eb + NH1 * hf * 128 * D,
                            [[D, 128], [128 * D, NH1], [1, 512]]), ("wd", sl))

            esl = contextlib.ExitStack()
            with esl:
                esg = contextlib.ExitStack()
                with esg:
                    gts0 = sb(esg, nc, "gts0", [128, 512], F32)
                    gts1 = sb(esg, nc, "gts1", [128, 512], F32)
                    pg0 = pt(esg, nc, "pg0", [128, 512])
                    pg1 = pt(esg, nc, "pg1", [128, 512])
                    pu0 = pt(esg, nc, "pu0", [128, 512])
                    pu1 = pt(esg, nc, "pu1", [128, 512])
                    gts, pgs, pus = [gts0, gts1], [pg0, pg1], [pu0, pu1]

                    twg = {0: pre["twg0"]}
                    issue_wd(0)
                    issue_wd(1)
                    if stg == 0:
                        io.waits("sync", [toks.get("sdb_w")])
                        toks["tsdb"] = io.dma(
                            "sync", sdb[:],
                            bass.AP(T["sdi"], 0, [[0, 128], [1, D]]), "consts")
                    io.waits("tensor", pre["t_xn"])
                    tdv = None
                    for fc in range(NFC):
                        p = fc % 2
                        if fc + 1 < NFC:
                            toks[("wgu", (fc + 1) % 2)] = io.now("pe")
                            twg[fc + 1] = issue_fc(fc + 1)
                        for tb in range(FT // 512):
                            io.waits("tensor", twg[fc] + [toks.get(("pgu", tb))])
                            for kc in range(NKC):
                                t.matmul(pgs[tb][:], wgs[p][:, 128 * kc:128 * (kc + 1)],
                                         xn[:, kc * FT + 512 * tb:kc * FT + 512 * (tb + 1)],
                                         start=(kc == 0), stop=(kc == NKC - 1))
                            for kc in range(NKC):
                                mm = t.matmul(pus[tb][:], wus[p][:, 128 * kc:128 * (kc + 1)],
                                              xn[:, kc * FT + 512 * tb:kc * FT + 512 * (tb + 1)],
                                              start=(kc == 0), stop=(kc == NKC - 1))
                            tpe = io.mark_mm(mm)
                            io.waits("scalar", [tpe, toks.get(("gts", tb))])
                            s.activation(gts[tb][:], pgs[tb][:], AF.Silu, bias=0.0,
                                         scale=T["sgc"][:, fc:fc + 1])
                            tac = io.mark("scalar")
                            io.wait("vector", tac)
                            v.scalar_tensor_tensor(
                                hh[:, fc * FT + 512 * tb:fc * FT + 512 * (tb + 1)],
                                pus[tb][:], T["suc"][:, fc:fc + 1], gts[tb][:],
                                ALU.mult, ALU.mult)
                            tdv = io.mark("vector")
                            toks[("pgu", tb)] = tdv
                            toks[("gts", tb)] = tdv
                    thh = tdv
            io.barrier()

            esd = contextlib.ExitStack()
            with esd:
                pt0 = sb(esd, nc, "ptile0", [128, 512], F32)
                pt1 = sb(esd, nc, "ptile1", [128, 512], F32)
                xe0 = sb(esd, nc, "xev0", [128, 512], F32)
                xe1 = sb(esd, nc, "xev1", [128, 512], F32)
                ot0 = sb(esd, nc, "otile0", [128, 512], F32)
                ot1 = sb(esd, nc, "otile1", [128, 512], F32)
                NTCS = FT // 128
                pds = [pt(esd, nc, f"pdt{i}", [128, 512]) for i in range(NTCS)]
                pts, xes, ots = [pt0, pt1], [xe0, xe1], [ot0, ot1]
                tsdb = toks["tsdb"]
                if stg + 1 < NFST:
                    toks[("wgu", 0)] = io.now("pe")
                    toks[("wgu", 1)] = io.now("pe")
                    pre["t_xn"] = load_xn(stg + 1, war=thh)
                    pre["twg0"] = issue_fc(0)
                for eb in range(4):
                    io.waits("tensor", [toks[("wdld", 2 * eb)], thh])
                    for tc in range(NTCS):
                        io.waits("tensor", [toks.get(("pdev", tc))])
                        for fc in range(NH1):
                            mm = t.matmul(pds[tc][:],
                                          hh[:, fc * FT + 128 * tc:fc * FT + 128 * (tc + 1)],
                                          wdh[0][:, 512 * fc:512 * (fc + 1)],
                                          start=(fc == 0), stop=False)
                    toks[("wdslot", 0)] = io.mark_mm(mm)
                    if eb + 1 < 4:
                        issue_wd(2 * eb + 2)
                    io.waits("tensor", [toks[("wdld", 2 * eb + 1)]])
                    for tc in range(NTCS):
                        i = tc % 2
                        trow = t0 + 128 * tc
                        io.waits("sync", [toks.get(("ptile", i)), toks.get(("xev", i)),
                                          toks.get(("partial_w", seq))])
                        tp_ = io.dma("sync", pts[i][:],
                                     ps[trow:trow + 128, 512 * eb:512 * (eb + 1)], ("pt", i))
                        tx_ = io.dma("sync", xes[i][:],
                                     xs[trow:trow + 128, 512 * eb:512 * (eb + 1)], ("xe", i))
                        for fc in range(NH1, NFC):
                            mm = t.matmul(pds[tc][:],
                                          hh[:, fc * FT + 128 * tc:fc * FT + 128 * (tc + 1)],
                                          wdh[1][:, 512 * (fc - NH1):512 * (fc - NH1) + 512],
                                          start=False, stop=(fc == NFC - 1))
                        tpe = io.mark_mm(mm)
                        io.waits("vector", [tpe, tp_, toks.get(("ot_po", i)),
                                            toks.get(("otile", i)), tsdb])
                        v.tensor_tensor(ots[i][:], pds[tc][:],
                                        sdb[:, 512 * eb:512 * (eb + 1)], ALU.mult)
                        v.drain()
                        v.tensor_tensor(ots[i][:], ots[i][:], pts[i][:], ALU.add)
                        tdv = io.mark("vector")
                        toks[("ptile", i)] = tdv
                        toks[("pdev", tc)] = tdv
                        io.waits("gpsimd", [tdv, tx_])
                        g.tensor_tensor(ots[i][:], ots[i][:], xes[i][:], ALU.add)
                        tpo = io.mark("gpsimd")
                        toks[("ot_po", i)] = tpo
                        toks[("xev", i)] = tpo
                        io.wait("sync", tpo)
                        toks[("otile", i)] = io.dma(
                            "sync", os_[trow:trow + 128, 512 * eb:512 * (eb + 1)], ots[i][:],
                            ("ot", i))
                    toks[("wdslot", 1)] = io.now("pe")
                    if eb + 1 < 4:
                        issue_wd(2 * eb + 3)
        io.barrier()


def build_program():
    nc = bass.Bass("TRN2", target_bir_lowering=False)

    x = nc.dram_tensor("x", [NSEQ, S, D], F32, kind="ExternalInput")
    out = nc.dram_tensor("out", [NSEQ, S, D], F32, kind="ExternalOutput")
    wqT = nc.dram_tensor("wqT", [D, D], F32, kind="ExternalInput")
    wkT = nc.dram_tensor("wkT", [D, D], F32, kind="ExternalInput")
    wvT = nc.dram_tensor("wvT", [D, D], F32, kind="ExternalInput")
    woT = nc.dram_tensor("woT", [D, D], F32, kind="ExternalInput")
    U8 = mybir.dt.uint8
    wgT = nc.dram_tensor("wgT", [D, FF], U8, kind="ExternalInput")
    wuT = nc.dram_tensor("wuT", [D, FF], U8, kind="ExternalInput")
    wdT = nc.dram_tensor("wdT", [FF, D], U8, kind="ExternalInput")
    memT = nc.dram_tensor("memT", [H, HD, M, HD], F32, kind="ExternalInput")
    mnT = nc.dram_tensor("mnT", [H, HD, M], F32, kind="ExternalInput")
    gatei = nc.dram_tensor("gatei", [1, H], F32, kind="ExternalInput")
    sgi = nc.dram_tensor("sgi", [128, NFC], F32, kind="ExternalInput")
    sui = nc.dram_tensor("sui", [128, NFC], F32, kind="ExternalInput")
    sdi = nc.dram_tensor("sdi", [1, D], F32, kind="ExternalInput")
    maski = nc.dram_tensor("maski", [128, 128], F32, kind="ExternalInput")
    identi = nc.dram_tensor("identi", [128, 128], BF16, kind="ExternalInput")

    wgB = nc.dram_tensor("wgB", [D, FF], BF16, kind="Internal")
    wuB = nc.dram_tensor("wuB", [D, FF], BF16, kind="Internal")
    wdB = nc.dram_tensor("wdB", [FF, D], BF16, kind="Internal")
    partial = nc.dram_tensor("partial", [NSEQ, S, D], F32, kind="Internal")
    xnF = nc.dram_tensor("xnF", [NSEQ, D, S], BF16, kind="Internal")

    est = contextlib.ExitStack()
    dma_pool = [f"d{i:02d}" for i in range(32)]
    sem_names = ["pe", "dv", "ac", "po", "bar"] + dma_pool
    sems = {n: est.enter_context(nc.semaphore(n)) for n in sem_names}
    io = Ctx(nc, sems, dma_pool)
    t, v, s, g, y = nc.tensor, nc.vector, nc.scalar, nc.gpsimd, nc.sync
    toks = {}

    with est:
        ident = sb(est, nc, "ident", [128, 128], BF16)
        mask = sb(est, nc, "mask", [128, 128], F32)
        ones_bf = sb(est, nc, "ones_bf", [128, 1], BF16)
        onesr_f = sb(est, nc, "onesr_f", [1, 128], F32)
        ones1_f = sb(est, nc, "ones1_f", [1, 1], F32)
        onesc_f = sb(est, nc, "onesc_f", [128, 1], F32)
        epsb = sb(est, nc, "epsb", [128, 1], F32)
        gsb = sb(est, nc, "gsb", [1, H], F32)
        gsig = sb(est, nc, "gsig", [1, H], F32)
        g1b = sb(est, nc, "g1b", [128, H], F32)
        mnsb = sb(est, nc, "mnsb", [128, H * M], F32)
        mnb16 = sb(est, nc, "mnb16", [128, H * M], BF16)
        ne4 = sb(est, nc, "ne4", [1, M], F32)
        sgc = sb(est, nc, "sgc", [128, NFC], F32)
        suc = sb(est, nc, "suc", [128, NFC], F32)
        smp = sb(est, nc, "smp", [1, 128], F32)
        cvb_i = [sb(est, nc, f"cvb_i{i}", [128, 512], BF16) for i in range(2)]
        cvb_o = [sb(est, nc, f"cvb_o{i}", [128, 512], BF16) for i in range(2)]
        est.enter_context(nc.Block())
        if True:
            T = dict(xs=x.ap().rearrange("b s d -> (b s) d"),
                     os=out.ap().rearrange("b s d -> (b s) d"),
                     ps=partial.ap().rearrange("b s d -> (b s) d"),
                     wqT=wqT, wkT=wkT, wvT=wvT, woT=woT, wgB=wgB, wuB=wuB, wdB=wdB,
                     memT=memT, ident=ident, mask=mask, ones_bf=ones_bf,
                     onesr_f=onesr_f, ones1_f=ones1_f, epsb=epsb, gsig=gsig,
                     g1b=g1b, mnb16=mnb16, ne4=ne4, smp=smp, sgc=sgc, suc=suc,
                     xnF=xnF, tffnw=None, cv_step=lambda n=1: None)

            tld = [io.dma("sync", ident[:], identi.ap(), "consts"),
                   io.dma("sync", mask[:], maski.ap(), "consts"),
                   io.dma("sync", gsb[:], gatei.ap(), "consts"),
                   io.dma("sync", sgc[:], sgi.ap(), "consts"),
                   io.dma("sync", suc[:], sui.ap(), "consts"),
                   io.dma("sync", mnsb[:], mnT.ap().rearrange("h d m -> d h m"), "consts")]
            g.memset(ones_bf[:], 1.0)
            g.memset(onesr_f[:], 1.0)
            g.memset(ones1_f[:], 1.0)
            g.memset(onesc_f[:], 1.0)
            g.memset(epsb[:], LN_EPS)
            tpo = io.mark("gpsimd")
            io.waits("scalar", tld + [tpo])
            s.activation(gsig[:], gsb[:], AF.Sigmoid, bias=0.0, scale=1.0)
            tac = io.mark("scalar")
            io.waits("vector", [tac] + tld)
            v.tensor_scalar(smp[0:1, 0:H], gsig[:], -1.0, 1.0, ALU.mult, ALU.add)
            v.tensor_copy(mnb16[:], mnsb[:])
            tdv = io.mark("vector")

            esp = contextlib.ExitStack()
            with esp:
                pbc = pt(esp, nc, "pbc", [128, 512])
                io.wait("tensor", tdv)
                mm = t.matmul(pbc[:, 0:H], onesr_f[:], smp[0:1, 0:H], start=True, stop=True)
                tpe = io.mark_mm(mm)
                io.wait("vector", tpe)
                v.tensor_copy(g1b[:], pbc[:, 0:H])
                tg1b = io.mark("vector")
                io.waits("tensor", [tg1b])
                mm = t.matmul(pbc[0:1, 64:128], onesc_f[:], mnsb[:], start=True, stop=True)
                tpe2 = io.mark_mm(mm)
                io.wait("vector", tpe2)
                v.tensor_copy(smp[0:1, 16:80], pbc[0:1, 64:128])
                v.drain()
                v.tensor_reduce(ne4[:], bass.AP(ap_of(smp), 16, [[128, 1], [1, 4], [4, 16]]),
                                AX.X, ALU.add)
                v.drain()
                v.tensor_single_scalar(ne4[:], ne4[:], EPS, ALU.is_ge)
                tne = io.mark("vector")

            io.barrier()

            cvq = []
            if DO_FFN:
                tiles = []
                for mat, dst in ((wgT, wgB), (wuT, wuB)):
                    for rt in range(16):
                        for st_ in range(11):
                            tiles.append((mat, dst, rt, 512 * st_))
                for rt in range(NFC):
                    for st_ in range(4):
                        tiles.append((wdT, wdB, rt, 512 * st_))
                cvst = {"load": 0, "proc": 0, "tin": {}}

                def cv_load(i):
                    mat, dst, rt, c0 = tiles[i]
                    par = i % 2
                    io.waits("gpsimd", [toks.get(("cvi_war", par))])
                    cvst["tin"][i] = io.dma(
                        "gpsimd", cvb_i[par][:],
                        mat.ap()[128 * rt:128 * (rt + 1), c0:c0 + 512], ("cvi", par))

                def cv_proc(i):
                    mat, dst, rt, c0 = tiles[i]
                    par = i % 2
                    io.waits("scalar", [cvst["tin"].pop(i), toks.get(("cvo_w", par))])
                    s.activation(cvb_o[par][:], cvb_i[par][:], AF.Copy,
                                 bias=-128.0, scale=1.0)
                    tcv = io.mark("scalar")
                    toks[("cvi_war", par)] = tcv
                    io.wait("sync", tcv)
                    toks[("cvo_w", par)] = io.dma(
                        "sync", dst.ap()[128 * rt:128 * (rt + 1), c0:c0 + 512],
                        cvb_o[par][:], ("cvo", par))

                def cv_tick():
                    while cvst["load"] < min(cvst["proc"] + 2, len(tiles)):
                        cv_load(cvst["load"])
                        cvst["load"] += 1
                    if cvst["proc"] < cvst["load"]:
                        cv_proc(cvst["proc"])
                        cvst["proc"] += 1
                        return True
                    return False

                for _ in tiles:
                    cvq.append(cv_tick)

            def cv_step(n=1):
                for _ in range(n):
                    if cvq:
                        cvq.pop(0)()
            T["cv_step"] = cv_step
            T["cvq"] = cvq
            T["sdi"] = sdi

            for seq in range(NSEQ):
                if DO_ATTN:
                    emit_attn_seq(io, toks, seq, T)
                while cvq:
                    cv_step()
                if T.get("tffnw") is None:
                    T["tffnw"] = [io.now_ch(("cvo", 0)), io.now_ch(("cvo", 1)),
                                  io.now_ch(("cvi", 0)), io.now_ch(("cvi", 1))]
                if DO_FFN:
                    emit_ffn_seq(io, toks, seq, T)

            io.barrier()
            for ch, q in io.chan.items():
                if io.ct[q] > 0:
                    y.wait_ge(sems[q], io.ct[q])

    nc.finalize()
    return nc


_nc_cache = {}


def _get_program():
    key = (DO_ATTN, DO_FFN)
    if key not in _nc_cache:
        _nc_cache[key] = build_program()
    return _nc_cache[key]


def kernel(L0_attn_hidden_states, L0_attn_memories, L0_attn_memory_norms,
           L0_attn_w_q, L0_attn_w_k, L0_attn_w_v, L0_attn_w_o, L0_attn_gate,
           L0_ffn_wq_gate, L0_ffn_s_gate, L0_ffn_wq_up, L0_ffn_s_up,
           L0_ffn_wq_down, L0_ffn_s_down):
    x = np.asarray(L0_attn_hidden_states, np.float32)
    co = np.ascontiguousarray
    shared = {
        "wqT": co(np.asarray(L0_attn_w_q, np.float32).T),
        "wkT": co(np.asarray(L0_attn_w_k, np.float32).T),
        "wvT": co(np.asarray(L0_attn_w_v, np.float32).T),
        "woT": co(np.asarray(L0_attn_w_o, np.float32).T),
        "wgT": co(np.asarray(L0_ffn_wq_gate, np.int32).T.astype(np.uint8)),
        "wuT": co(np.asarray(L0_ffn_wq_up, np.int32).T.astype(np.uint8)),
        "wdT": co(np.asarray(L0_ffn_wq_down, np.int32).T.astype(np.uint8)),
        "memT": co(np.transpose(np.asarray(L0_attn_memories, np.float32), (1, 2, 0, 3))),
        "mnT": co(np.transpose(np.asarray(L0_attn_memory_norms, np.float32), (1, 2, 0))),
        "gatei": co(np.asarray(L0_attn_gate, np.float32).reshape(1, H)),
        "sgi": co(np.asarray(L0_ffn_s_gate, np.float32).reshape(NFC, 128).T),
        "sui": co(np.asarray(L0_ffn_s_up, np.float32).reshape(NFC, 128).T),
        "sdi": co(np.asarray(L0_ffn_s_down, np.float32).reshape(1, D)),
        "maski": co(np.triu(np.ones((128, 128), np.float32))),
        "identi": np.eye(128, dtype=BFD),
    }
    in_maps = []
    for c in range(8):
        m = dict(shared)
        m["x"] = co(x[NSEQ * c:NSEQ * (c + 1)])
        in_maps.append(m)
    nc = _get_program()
    res = run_bass_kernel_spmd(nc, in_maps, core_ids=list(range(8)))
    return np.concatenate([r["out"] for r in res.results], axis=0)


if __name__ == "__main__":
    nc = _get_program()
    print("instructions:", len(list(nc.all_instructions())))


# version 24
# speedup vs baseline: 1.0897x; 1.0257x over previous
import numpy as np
import ml_dtypes
import contextlib

import concourse.bass as bass
import concourse.mybir as mybir
from concourse.bass_utils import run_bass_kernel_spmd

F32 = mybir.dt.float32
BF16 = mybir.dt.bfloat16
I32 = mybir.dt.int32
AF = mybir.ActivationFunctionType
ALU = mybir.AluOpType
AX = mybir.AxisListType

BFD = ml_dtypes.bfloat16

D = 2048
H = 16
HD = 128
M = 4
FF = 5632
S = 2048
NSEQ = 2
NKC = D // 128
NTC = S // 128
NFC = FF // 128
FT = 1024
NFST = S // FT
SCALE = float(HD) ** -0.5
EPS = 1e-6
LN_EPS = 1e-5

DO_ATTN = True
DO_FFN = True


def ap_of(buf):
    return buf.handle if hasattr(buf, "handle") else buf


_uid = [0]

def sb(es, nc, name, shape, dtype):
    _uid[0] += 1
    return es.enter_context(nc.sbuf_tensor(f"{name}_{_uid[0]}", shape, dtype))

def pt(es, nc, name, shape, dtype=F32):
    _uid[0] += 1
    return es.enter_context(nc.psum_tensor(f"{name}_{_uid[0]}", shape, dtype))


class Ctx:

    def __init__(self, nc, sems, dma_pool):
        self.nc = nc
        self.sems = sems
        self.ct = {k: 0 for k in sems}
        self.dma_pool = list(dma_pool)
        self.chan = {}
        self.settled = {}

    def _ch(self, channel):
        if channel not in self.chan:
            if not self.dma_pool:
                raise RuntimeError("out of DMA semaphores")
            self.chan[channel] = self.dma_pool.pop(0)
        return self.chan[channel]

    def dma(self, eng, dst, src, channel, burst=False):
        q = self._ch(channel)
        e = getattr(self.nc, eng)
        if not burst and self.ct[q] > self.settled.get(q, 0):
            e.wait_ge(self.sems[q], self.ct[q])
            self.settled[q] = self.ct[q]
        e.dma_start(dst, src).then_inc(self.sems[q], 16)
        self.ct[q] += 16
        return (q, self.ct[q])

    def mark(self, eng):
        n = {"vector": "dv", "scalar": "ac", "gpsimd": "po"}[eng]
        getattr(self.nc, eng).drain().then_inc(self.sems[n], 1)
        self.ct[n] += 1
        return (n, self.ct[n])

    def mark_mm(self, instr):
        instr.then_inc(self.sems["pe"], 1)
        self.ct["pe"] += 1
        return ("pe", self.ct["pe"])

    def waits(self, eng, toks):
        best = {}
        flat = []
        for t in toks:
            if t is None:
                continue
            if isinstance(t, list):
                flat.extend(x for x in t if x is not None)
            else:
                flat.append(t)
        for n, v in flat:
            best[n] = max(best.get(n, 0), v)
        for n, v in best.items():
            getattr(self.nc, eng).wait_ge(self.sems[n], v)

    def wait(self, eng, tok):
        self.waits(eng, [tok])

    def now(self, name):
        return (name, self.ct[name])

    def now_ch(self, channel):
        if channel not in self.chan:
            return None
        q = self.chan[channel]
        return (q, self.ct[q])

    def barrier(self):
        for ch, q in self.chan.items():
            if self.ct[q] > self.settled.get(q, 0):
                self.nc.sync.wait_ge(self.sems[q], self.ct[q])
                self.settled[q] = self.ct[q]
        bar = self.sems["bar"]
        base = self.ct["bar"]
        for eng in ("vector", "scalar", "gpsimd"):
            getattr(self.nc, eng).drain().then_inc(bar, 1)
        self.nc.tensor.nop().then_inc(bar, 1)
        self.nc.sync.nop().then_inc(bar, 1)
        self.ct["bar"] += 5
        for eng in ("vector", "scalar", "gpsimd", "tensor", "sync"):
            getattr(self.nc, eng).wait_ge(bar, base + 5)


def emit_ln_chunk(c, nchunks, io, bufs, toks, xsrc, col_of_kc, stage_w, sxn_psums):
    nc = io.nc
    v, s, t, g = nc.vector, nc.scalar, nc.tensor, nc.gpsimd
    xt = bufs["xtile"][c % 2]
    jA, jB, st1 = bufs["junkA"], bufs["junkB"], bufs["stat1"]
    xnt = bufs["xn_tm"][c % 2]
    tp = bufs["tpsum"]

    io.waits("sync", [toks.get(("xtile", c % 2))])
    ld = io.dma("sync", xt[:], xsrc, ("xt", c % 2))
    o = 8 * (c % 2)
    io.waits("scalar", [ld, toks.get(("st1war", c % 2))])
    s.activation(jA[:], xt[:], AF.Square, bias=0.0, scale=1.0,
                 accum_out=st1[:, o + 1:o + 2])
    s.activation(jB[:, 1024:2048], xt[:, 0:1024], AF.Identity, bias=0.0, scale=1.0,
                 accum_out=st1[:, o + 7:o + 8])
    tac = io.mark("scalar")
    io.wait("vector", ld)
    v.tensor_scalar(jB[:, 0:1024], xt[:, 1024:2048], 0.0, 0.0, ALU.add, ALU.add,
                    accum_out=st1[:, o:o + 1])
    io.wait("vector", tac)
    v.drain()
    v.tensor_tensor(st1[:, o + 6:o + 7], st1[:, o:o + 1], st1[:, o + 7:o + 8], ALU.add)
    v.drain()
    v.tensor_scalar_mul(st1[:, o + 2:o + 3], st1[:, o + 6:o + 7], 1.0 / 2048.0)
    v.drain()
    v.tensor_mul(st1[:, o + 3:o + 4], st1[:, o + 2:o + 3], st1[:, o + 2:o + 3])
    io.wait("vector", tac)
    v.drain()
    v.scalar_tensor_tensor(st1[:, o + 4:o + 5], st1[:, o + 1:o + 2], 1.0 / 2048.0,
                           st1[:, o + 3:o + 4], ALU.mult, ALU.subtract)
    tvar = io.mark("vector")
    toks[("st1war", c % 2)] = tvar
    io.wait("scalar", tvar)
    s.activation(st1[:, o + 6:o + 7], st1[:, o + 4:o + 5], AF.Ln,
                 bias=bufs["epsb"][:], scale=1.0)
    s.drain()
    s.activation(st1[:, o + 5:o + 6], st1[:, o + 6:o + 7], AF.Exp,
                 bias=0.0, scale=-0.5)
    trstd = io.mark("scalar")
    io.waits("vector", [trstd, toks.get(("xn_tm", c % 2))])
    v.tensor_scalar(xnt[:], xt[:], st1[:, o + 2:o + 3], st1[:, o + 5:o + 6],
                    ALU.subtract, ALU.mult)
    tnorm = io.mark("vector")
    toks[("xtile", c % 2)] = tnorm
    for grp in range(4):
        pg = tp[grp % 2]
        io.waits("tensor", [tnorm, toks.get(("tpsum", grp % 2))])
        for j in range(4):
            kc = grp * 4 + j
            mm = t.transpose(pg[:, 128 * j:128 * (j + 1)],
                             xnt[:, 128 * kc:128 * (kc + 1)], bufs["ident"][:])
        tpe = io.mark_mm(mm)
        eng = "vector" if grp % 2 == 0 else "scalar"
        io.wait(eng, tpe)
        dst = bass.AP(ap_of(bufs["xn"]), col_of_kc(grp * 4),
                      [[bufs["xn_psz"], 128], [stage_w, 4], [1, 128]])
        if eng == "vector":
            v.tensor_copy(dst, pg[:])
        else:
            s.copy(dst, pg[:])
        toks[("tpsum", grp % 2)] = io.mark(eng)
    if sxn_psums is not None:
        for q4 in range(4):
            mm = t.matmul(sxn_psums[q4][:], bufs["ones_bf"][:],
                          xnt[:, 512 * q4:512 * (q4 + 1)],
                          start=(c == 0), stop=(c == nchunks - 1))
        if c == nchunks - 1:
            toks["sxn"] = io.mark_mm(mm)
        toks[("xn_tm", c % 2)] = ("pe", io.ct["pe"] + (0 if c == nchunks - 1 else 1))
        if c < nchunks - 1:
            io.mark_mm(mm)
    else:
        toks[("xn_tm", c % 2)] = toks[("tpsum", 1)]


def emit_attn_seq(io, toks, seq, T):
    nc = io.nc
    t, v, s, g, y = nc.tensor, nc.vector, nc.scalar, nc.gpsimd, nc.sync
    xs, ps, os_ = T["xs"], T["ps"], T["os"]
    r0 = seq * S

    es0 = contextlib.ExitStack()
    with es0:
        xn = sb(es0, nc, "xn_a", [128, NKC * S], BF16)
        mix = sb(es0, nc, "mix", [128, NTC * D], BF16)
        xbarT = sb(es0, nc, "xbarT", [128, NKC], BF16)
        wppb = sb(es0, nc, "wppb", [128, M], F32)
        es1 = contextlib.ExitStack()
        with es1:
            xtA = sb(es1, nc, "xtileA", [128, D], F32)
            xtB = sb(es1, nc, "xtileB", [128, D], F32)
            jA = sb(es1, nc, "junkA", [128, D], F32)
            jB = sb(es1, nc, "junkB", [128, D], F32)
            st1 = sb(es1, nc, "stat1", [128, 16], F32)
            xnA = sb(es1, nc, "xn_tmA", [128, D], BF16)
            xnB = sb(es1, nc, "xn_tmB", [128, D], BF16)
            sxn_sb = sb(es1, nc, "sxn_sb", [1, D], F32)
            tpA = pt(es1, nc, "tpA", [128, 512], BF16)
            tpB = pt(es1, nc, "tpB", [128, 512], BF16)
            sx0 = pt(es1, nc, "sx0", [1, 512])
            sx1 = pt(es1, nc, "sx1", [1, 512])
            sx2 = pt(es1, nc, "sx2", [1, 512])
            sx3 = pt(es1, nc, "sx3", [1, 512])
            pxb = pt(es1, nc, "pxb", [128, 16])
            bufs = dict(T)
            bufs.update(xn=xn, xn_psz=NKC * S, xtile=[xtA, xtB], junkA=jA, junkB=jB,
                        stat1=st1, xn_tm=[xnA, xnB], tpsum=[tpA, tpB])
            sxp = [sx0, sx1, sx2, sx3]
            for c in range(NTC):
                emit_ln_chunk(c, NTC, io, bufs, toks, xs[r0 + 128 * c:r0 + 128 * (c + 1), :],
                              lambda kc, c=c: kc * S + 128 * c, S, sxp)
                T["cv_step"](1)
            io.wait("vector", toks["sxn"])
            for q4 in range(4):
                v.tensor_copy(sxn_sb[:, 512 * q4:512 * (q4 + 1)], sxp[q4][:])
            tsxe = io.mark("vector")
            io.wait("tensor", tsxe)
            for kc in range(NKC):
                mm = t.matmul(pxb[:, kc:kc + 1], sxn_sb[0:1, 128 * kc:128 * (kc + 1)],
                              T["ones1_f"][:], start=True, stop=True)
            tpe = io.mark_mm(mm)
            io.wait("vector", tpe)
            v.tensor_scalar_mul(xbarT[:], pxb[:], 1.0 / 2048.0)
            toks["xbarT"] = io.mark("vector")
            io.waits("sync", [io.now("dv"), io.now("ac")])
            toks[("xnf", seq)] = io.dma(
                "sync", bass.AP(T["xnF"], seq * D * S, [[S, 128], [128 * S, NKC], [1, S]]),
                xn[:].rearrange("p (c t) -> p c t", c=NKC), "xnf")
        io.barrier()

        es2 = contextlib.ExitStack()
        with es2:
            wq0 = sb(es2, nc, "wq0", [128, NKC * 128], BF16)
            wq1 = sb(es2, nc, "wq1", [128, NKC * 128], BF16)
            wk0 = sb(es2, nc, "wk0", [128, NKC * 128], BF16)
            wv0 = sb(es2, nc, "wv0", [128, NKC * 128], BF16)
            mem0 = sb(es2, nc, "mem0", [128, M * 128], BF16)
            mem1 = sb(es2, nc, "mem1", [128, M * 128], BF16)
            mem2 = sb(es2, nc, "mem2", [128, M * 128], BF16)
            sqb_ = [sb(es2, nc, f"sq{i}", [128, S], BF16) for i in range(2)]
            skb_ = [sb(es2, nc, f"sk{i}", [128, S], BF16) for i in range(2)]
            skt0 = sb(es2, nc, "sktm0", [128, S], BF16)
            sktb = [skt0, skt0]
            vtmb = [sb(es2, nc, f"vtm{i}", [128, S], BF16) for i in range(2)]
            Sball = [sb(es2, nc, f"Sball{i}", [128, NTC * 133], BF16) for i in range(2)]
            wppbb = [sb(es2, nc, f"wppb{i}", [128, 8], F32) for i in range(2)]
            vst0 = sb(es2, nc, "vst0", [128, 512], BF16)
            vst1 = sb(es2, nc, "vst1", [128, 512], BF16)
            vstb = [vst0, vst1]
            el0 = sb(es2, nc, "el0", [128, 512], BF16)
            el1 = sb(es2, nc, "el1", [128, 512], BF16)
            el2 = sb(es2, nc, "el2", [128, 512], BF16)
            el3 = sb(es2, nc, "el3", [128, 512], BF16)
            msc0 = sb(es2, nc, "msc0", [128, 128], BF16)
            msc1 = sb(es2, nc, "msc1", [128, 128], BF16)
            mm640a = sb(es2, nc, "mm640a", [128, 640], F32)
            mm640b = sb(es2, nc, "mm640b", [128, 640], F32)
            den5 = sb(es2, nc, "den5", [128, 8], F32)
            betab = sb(es2, nc, "betab", [128, 8], F32)
            qbarb = sb(es2, nc, "qbarb", [128, 1], BF16)
            pA = pt(es2, nc, "pA", [128, 1024])
            pKV = pt(es2, nc, "pKV", [128, 129])
            pU = pt(es2, nc, "pU", [128, 512])
            pDNa = pt(es2, nc, "pDNa", [128, 136])
            pDNb = pt(es2, nc, "pDNb", [128, 136])
            pSC2 = pt(es2, nc, "pSC2", [128, 256])
            pD = pt(es2, nc, "pD", [128, 512], BF16)
            wqb, wkb, wvb = [wq0, wq1], [wk0, wk0], [wv0, wv0]
            memb = [mem0, mem1, mem2]
            mscb = [msc0, msc1]
            m6b = [mm640a, mm640b]
            rl = T["smp"]

            def issue_head_weights(h):
                p, p3 = h % 2, h % 3
                wt_toks = []
                io.waits("gpsimd", [toks.get(("wslab", p)), toks.get(("memslab", p3))])
                for wi, (wt, bb, ch) in enumerate(((T["wqT"], wqb[p], ("wh", p)),
                                                   (T["wkT"], wkb[p], "whk"),
                                                   (T["wvT"], wvb[p], "whv"))):
                    src = bass.AP(wt, 128 * h, [[D, 128], [128 * D, NKC], [1, 128]])
                    dst = bb[:].rearrange("p (c e) -> p c e", c=NKC)
                    wt_toks.append(io.dma("gpsimd", dst, src, ch))
                wt_toks.append(io.dma("gpsimd",
                                      memb[p3][:].rearrange("p (m e) -> p m e", m=M),
                                      T["memT"].ap()[h], ("mem", p3)))
                return wt_toks

            def sigma_evac(blk, dst):
                tb = blk % 2
                ea = el0 if tb == 0 else el1
                eb = el2 if tb == 0 else el3
                col = 512 * blk
                tpe = toks[("projmm", tb)]
                io.waits("scalar", [tpe, toks.get(("el", tb))])
                s.activation(ea[:], pA[:, 512 * tb:512 * (tb + 1)], AF.Exp,
                             bias=0.0, scale=1.0)
                s.activation(eb[:], pA[:, 512 * tb:512 * (tb + 1)], AF.Relu,
                             bias=0.0, scale=1.0)
                tac = io.mark("scalar")
                io.wait("vector", tac)
                v.scalar_tensor_tensor(dst[:, col:col + 512], ea[:], 1.0,
                                       eb[:], ALU.min, ALU.add)
                tdv = io.mark("vector")
                toks[("pA", tb)] = tac
                toks[("el", tb)] = tdv
                return tdv

            pbq = []

            def pb_step(n=1):
                T["cv_step"](2)
                for _ in range(n):
                    if pbq:
                        pbq.pop(0)()

            pSCb_ = [pSC2[:, 0:128], pSC2[:, 128:256]]
            pDNp = [pDNa, pDNb]

            def make_phaseB(hh, twh):
                ph, p3 = hh % 2, hh % 3
                sq, sk = sqb_[ph], skb_[ph]
                v_tm = vtmb[ph]
                Sb_all = Sball[ph]
                wppb = wppbb[ph]

                def first():
                    io.waits("tensor", [toks[("sq", ph)], toks[("sk", ph)],
                                        toks.get(("pSC_free", 0))])
                    mm = t.matmul(pSCb_[0], sk[:, 0:128], sq[:, 0:128],
                                  start=True, stop=True)
                    toks[("t_sc", 0)] = io.mark_mm(mm)
                    io.waits("vector", [toks[("t_sc", 0)], toks.get(("msc_war", 0))])
                    v.tensor_tensor(mscb[0][:], pSCb_[0], T["mask"][:], ALU.mult)
                    tA = io.mark("vector")
                    toks[("msk", 0)] = tA
                    toks[("pSC_free", 0)] = tA

                def chunk(c):
                    col = 128 * c
                    mp = mscb[c % 2]
                    m6 = m6b[c % 2]
                    pDN = pDNp[c % 2]
                    io.waits("tensor", [toks[("msk", c)], toks.get(("pDN_free", c % 2)),
                                        toks[("Sslot", ph, c)]])
                    t.matmul(pDN[:, 0:5], sq[:, col:col + 128],
                             Sb_all[:, 133 * c + 128:133 * c + 133], start=True, stop=False)
                    t.matmul(pDN[:, 0:1], mp[:], T["ones_bf"][:], start=False, stop=False)
                    t.matmul(pDN[:, 8:136], sq[:, col:col + 128],
                             Sb_all[:, 133 * c:133 * c + 128], start=False, stop=False)
                    mm = t.matmul(pDN[:, 8:136], mp[:], v_tm[:, col:col + 128],
                                  start=False, stop=True)
                    t_mms = io.mark_mm(mm)
                    toks[("msc_war", c % 2)] = t_mms
                    if c + 1 < NTC:
                        io.waits("tensor", [toks.get(("pSC_free", (c + 1) % 2))])
                        mm = t.matmul(pSCb_[(c + 1) % 2], sk[:, col + 128:col + 256],
                                      sq[:, col + 128:col + 256], start=True, stop=True)
                        toks[("t_sc", c + 1)] = io.mark_mm(mm)
                        mpn = mscb[(c + 1) % 2]
                        io.waits("vector", [toks[("t_sc", c + 1)],
                                            toks.get(("msc_war", (c + 1) % 2))])
                        v.tensor_tensor(mpn[:], pSCb_[(c + 1) % 2], T["mask"][:], ALU.mult)
                        tA = io.mark("vector")
                        toks[("msk", c + 1)] = tA
                        toks[("pSC_free", (c + 1) % 2)] = tA
                    io.waits("tensor", [twh[3], toks.get("pU_free")])
                    mm = t.matmul(pU[:], sq[:, col:col + 128], memb[p3][:],
                                  start=True, stop=True)
                    t_u = io.mark_mm(mm)
                    if c + 1 == NTC:
                        toks[("memslab", p3)] = t_u
                    io.waits("vector", [t_mms, toks.get(("m6_war", c % 2))])
                    v.tensor_scalar_max(den5[:, 0:5], pDN[:, 0:5], EPS)
                    v.drain()
                    v.reciprocal(den5[:, 0:5], den5[:, 0:5])
                    v.drain()
                    v.tensor_mul(betab[:, 0:5], wppb[:, 0:5], den5[:, 0:5])
                    v.drain()
                    io.wait("vector", t_u)
                    bcast = bass.AP(ap_of(betab), 1, [[8, 128], [1, 4], [0, 128]])
                    v.tensor_tensor(m6[:, 0:512], pU[:], bcast, ALU.mult)
                    tB0 = io.mark("vector")
                    toks["pU_free"] = tB0
                    v.tensor_scalar_mul(m6[:, 512:640], pDN[:, 8:136], betab[:, 0:1])
                    tn = io.mark("vector")
                    toks[("pDN_free", c % 2)] = tn
                    with nc.allow_low_precision(reason="fp32-accumulated reduce, bf16 store"):
                        v.tensor_reduce(mix[:, c * D + 128 * hh:c * D + 128 * (hh + 1)],
                                        bass.AP(ap_of(m6), 0, [[640, 128], [1, 128], [128, 5]]),
                                        AX.X, ALU.add)
                    tB = io.mark("vector")
                    toks[("m6_war", c % 2)] = tB
                    toks["mix_done"] = tB

                out = [first]
                for c in range(NTC):
                    out.append(lambda c=c: chunk(c))
                return out

            def emit_wpp(h):
                p = h % 2
                io.waits("tensor", tw[h] + [toks.get("xbarT"), toks.get("pKV_free")])
                for kc in range(NKC):
                    mm = t.matmul(pKV[:, 0:1], wqb[p][:, 128 * kc:128 * (kc + 1)],
                                  xbarT[:, kc:kc + 1], start=(kc == 0), stop=(kc == NKC - 1))
                tpe = io.mark_mm(mm)
                io.wait("scalar", tpe)
                s.activation(qbarb[:], pKV[:, 0:1], AF.Copy, bias=0.0, scale=1.0)
                tqb = io.mark("scalar")
                pb_step()
                io.waits("tensor", [tqb])
                mm = t.matmul(pKV[0:1, 4:8], qbarb[:], T["mnb16"][:, M * h:M * (h + 1)],
                              start=True, stop=True)
                tpe = io.mark_mm(mm)
                io.wait("vector", tpe)
                v.tensor_scalar_mul(rl[0:1, 80:84], pKV[0:1, 4:8], SCALE)
                v.drain()
                v.tensor_reduce(rl[0:1, 84:85], rl[0:1, 80:84], AX.X, ALU.max)
                v.drain()
                v.tensor_scalar_mul(rl[0:1, 85:86], rl[0:1, 84:85], -1.0)
                tdv = io.mark("vector")
                io.wait("scalar", tdv)
                s.activation(rl[0:1, 88:92], rl[0:1, 80:84], AF.Exp,
                             bias=rl[0:1, 85:86], scale=1.0)
                tac = io.mark("scalar")
                io.wait("vector", tac)
                v.tensor_mul(rl[0:1, 88:92], rl[0:1, 88:92], T["ne4"][:])
                v.drain()
                v.tensor_reduce(rl[0:1, 92:93], rl[0:1, 88:92], AX.X, ALU.add)
                v.drain()
                v.tensor_scalar_max(rl[0:1, 93:94], rl[0:1, 92:93], 1e-30)
                v.drain()
                v.reciprocal(rl[0:1, 94:95], rl[0:1, 93:94])
                v.drain()
                v.tensor_scalar(rl[0:1, 96:100], rl[0:1, 88:92], rl[0:1, 94:95],
                                T["gsig"][0:1, h:h + 1], ALU.mult, ALU.mult)
                v.tensor_copy(rl[0:1, 95:96], rl[0:1, h:h + 1])
                tdv = io.mark("vector")
                pb_step()
                io.waits("tensor", [tdv])
                mm = t.matmul(pKV[:, 8:13], T["onesr_f"][:], rl[0:1, 95:100],
                              start=True, stop=True)
                tpe = io.mark_mm(mm)
                io.wait("vector", tpe)
                v.tensor_copy(wppbb[p][:, 0:5], pKV[:, 8:13])
                toks["pKV_free"] = io.mark("vector")

            tw = {0: issue_head_weights(0)}
            emit_wpp(0)
            for h in range(H):
                p, p3 = h % 2, h % 3
                sq, sk = sqb_[p], skb_[p]
                sk_tm, v_tm = sktb[p], vtmb[p]
                Sb_all = Sball[p]

                for blk in range(4):
                    tb = blk % 2
                    io.waits("tensor", [toks.get(("pA", tb))])
                    for kc in range(NKC):
                        mm = t.matmul(pA[:, 512 * tb:512 * (tb + 1)],
                                      wkb[p][:, 128 * kc:128 * (kc + 1)],
                                      xn[:, kc * S + 512 * blk:kc * S + 512 * (blk + 1)],
                                      start=(kc == 0), stop=(kc == NKC - 1))
                    toks[("projmm", tb)] = io.mark_mm(mm)
                    pb_step()
                    tdv = sigma_evac(blk, sk)
                toks[("sk", p)] = tdv
                for blk in range(4):
                    tb = blk % 2
                    vs = vstb[tb]
                    io.waits("tensor", [toks.get(("pA", tb))])
                    for kc in range(NKC):
                        mm = t.matmul(pA[:, 512 * tb:512 * (tb + 1)],
                                      wvb[p][:, 128 * kc:128 * (kc + 1)],
                                      xn[:, kc * S + 512 * blk:kc * S + 512 * (blk + 1)],
                                      start=(kc == 0), stop=(kc == NKC - 1))
                    tpe = io.mark_mm(mm)
                    pb_step()
                    io.waits("scalar", [tpe, toks.get(("vst", tb))])
                    s.copy(vs[:], pA[:, 512 * tb:512 * (tb + 1)])
                    tac = io.mark("scalar")
                    toks[("pA", tb)] = tac
                    io.waits("tensor", [tac, toks.get("pD_ev")])
                    for j in range(4):
                        mm = t.transpose(pD[:, 128 * j:128 * (j + 1)],
                                         vs[:, 128 * j:128 * (j + 1)], T["ident"][:])
                    tpe = io.mark_mm(mm)
                    toks[("vst", tb)] = tpe
                    io.wait("vector", tpe)
                    v.tensor_copy(v_tm[:, 512 * blk:512 * (blk + 1)], pD[:])
                    toks["pD_ev"] = io.mark("vector")
                io.waits("gpsimd", [io.now("pe"), io.now("dv"), io.now("ac")])
                g.memset(Sb_all[:, 0:129], 0.0)
                src_mn = bass.AP(ap_of(T["mnb16"]), 4 * h, [[64, 128], [0, NTC], [1, 4]])
                dst_mn = bass.AP(ap_of(Sb_all), 129, [[NTC * 133, 128], [133, NTC], [1, 4]])
                g.tensor_copy(dst_mn, src_mn)
                tSi = io.mark("gpsimd")
                toks[("Sslot", p, 0)] = tSi
                toks[("Scp", -1)] = tSi

                def scan_step(c):
                    col = 128 * c
                    io.waits("tensor", [toks[("Scp", c - 1)],
                                        toks.get(("sktmq", c // 4))])
                    t.matmul(pKV[:, 0:128], sk_tm[:, col:col + 128],
                             v_tm[:, col:col + 128], start=(c == 0), stop=False)
                    mm2 = t.matmul(pKV[:, 128:129], sk_tm[:, col:col + 128],
                                   T["ones_bf"][:], start=False,
                                   stop=(c == NTC - 2))
                    tkv = io.mark_mm(mm2)
                    eng = "scalar" if c % 2 == 0 else "vector"
                    io.waits(eng, [tkv])
                    if c % 2 == 0:
                        s.copy(Sb_all[:, 133 * (c + 1):133 * (c + 1) + 129], pKV[:])
                    else:
                        v.tensor_copy(Sb_all[:, 133 * (c + 1):133 * (c + 1) + 129],
                                      pKV[:])
                    tcp = io.mark(eng)
                    toks[("Scp", c)] = tcp
                    toks[("Sslot", p, c + 1)] = tcp

                scan_plan = {0: [0, 1], 1: [2, 3, 4], 2: [5, 6, 7], 3: [8, 9]}
                for q4 in range(4):
                    io.waits("tensor", [toks[("sk", p)], toks.get("pD_ev")])
                    for j in range(4):
                        c = 4 * q4 + j
                        mm = t.transpose(pD[:, 128 * j:128 * (j + 1)],
                                         sk[:, 128 * c:128 * (c + 1)], T["ident"][:])
                    tpe = io.mark_mm(mm)
                    if q4 % 2 == 1:
                        pb_step()
                    eng = "vector" if q4 % 2 == 0 else "scalar"
                    io.wait(eng, tpe)
                    if q4 % 2 == 0:
                        v.tensor_copy(sk_tm[:, 512 * q4:512 * (q4 + 1)], pD[:])
                    else:
                        s.copy(sk_tm[:, 512 * q4:512 * (q4 + 1)], pD[:])
                    toks["pD_ev"] = io.mark(eng)
                    toks[("sktmq", q4)] = toks["pD_ev"]
                    for c in scan_plan[q4]:
                        if c < NTC - 1:
                            scan_step(c)
                            if c % 2 == 1:
                                pb_step()
                tvtm = toks["pD_ev"]

                if h + 1 < H:
                    toks[("wslab", (h + 1) % 2)] = io.now("pe")
                    tw[h + 1] = issue_head_weights(h + 1)

                kv_cs = [c for c in range(10, NTC - 1)]
                for blk in range(4):
                    tb = blk % 2
                    io.waits("tensor", [toks.get(("pA", tb))])
                    for kc in range(NKC):
                        mm = t.matmul(pA[:, 512 * tb:512 * (tb + 1)],
                                      wqb[p][:, 128 * kc:128 * (kc + 1)],
                                      xn[:, kc * S + 512 * blk:kc * S + 512 * (blk + 1)],
                                      start=(kc == 0), stop=(kc == NKC - 1))
                        if kc == NKC - 1:
                            toks[("projmm", tb)] = io.mark_mm(mm)
                        if kc % 8 == 7 and kv_cs:
                            scan_step(kv_cs.pop(0))
                    pb_step()
                    tdv = sigma_evac(blk, sq)
                toks[("sq", p)] = tdv
                toks["pKV_free"] = toks[("Scp", NTC - 2)]

                pb_step(NTC + 1)
                pbq = make_phaseB(h, tw[h])
                if h + 1 < H:
                    emit_wpp(h + 1)
            pb_step(NTC + 1)
        io.barrier()
        io.barrier()

        es3 = contextlib.ExitStack()
        with es3:
            woeb = [sb(es3, nc, f"woeb{i}", [128, NKC * 512], BF16) for i in range(2)]
            ot0 = sb(es3, nc, "otile0", [128, 512], F32)
            ot1 = sb(es3, nc, "otile1", [128, 512], F32)
            xe0 = sb(es3, nc, "xev0", [128, 512], F32)
            xe1 = sb(es3, nc, "xev1", [128, 512], F32)
            gA = pt(es3, nc, "gA", [128, 256], BF16)
            gB = pt(es3, nc, "gB", [128, 256], BF16)
            wo0 = pt(es3, nc, "wo0", [128, 512])
            wo1 = pt(es3, nc, "wo1", [128, 512])
            pairs = [(a, b) for a in range(NTC) for b in range(a, NKC)]
            gp = [gA, gB]
            for i, (a, b) in enumerate(pairs):
                pg = gp[i % 2]
                io.waits("tensor", [toks.get(("gT", i % 2)), toks.get("mix_done")])
                mm = t.transpose(pg[:, 0:128], mix[:, a * D + 128 * b:a * D + 128 * (b + 1)],
                                 T["ident"][:])
                if b != a:
                    mm = t.transpose(pg[:, 128:256],
                                     mix[:, b * D + 128 * a:b * D + 128 * (a + 1)],
                                     T["ident"][:])
                tpe = io.mark_mm(mm)
                io.wait("vector", tpe)
                v.tensor_copy(mix[:, b * D + 128 * a:b * D + 128 * (a + 1)], pg[:, 0:128])
                if b != a:
                    v.tensor_copy(mix[:, a * D + 128 * b:a * D + 128 * (b + 1)], pg[:, 128:256])
                toks[("gT", i % 2)] = io.mark("vector")
            tgt = toks[("gT", (len(pairs) - 1) % 2)]
            wop, oti, xei = [wo0, wo1], [ot0, ot1], [xe0, xe1]

            def issue_wo(eb):
                src = bass.AP(T["woT"], 512 * eb, [[D, 128], [128 * D, NKC], [1, 512]])
                io.waits("gpsimd", [toks.get(("woslot", eb % 2))])
                toks[("wold", eb)] = io.dma(
                    "gpsimd", woeb[eb % 2][:].rearrange("p (c e) -> p c e", c=NKC),
                    src, ("wo", eb % 2))

            issue_wo(0)
            issue_wo(1)
            for eb in range(4):
                for tc in range(NTC):
                    i = tc % 2
                    io.waits("tensor", [toks[("wold", eb)], tgt, toks.get(("wo_ev", i))])
                    for j in range(NKC):
                        mm = t.matmul(wop[i][:], mix[:, j * D + 128 * tc:j * D + 128 * (tc + 1)],
                                      woeb[eb % 2][:, 512 * j:512 * (j + 1)],
                                      start=(j == 0), stop=(j == NKC - 1))
                    tpe = io.mark_mm(mm)
                    if DO_FFN:
                        io.waits("scalar", [tpe, toks.get(("otile", i))])
                        s.activation(oti[i][:], wop[i][:], AF.Copy, bias=0.0, scale=1.0)
                        tev = io.mark("scalar")
                        dstt = ps
                    else:
                        io.waits("sync", [toks.get(("xev", i))])
                        tx = io.dma("sync", xei[i][:],
                                    xs[r0 + 128 * tc:r0 + 128 * (tc + 1), 512 * eb:512 * (eb + 1)],
                                    ("xe", i))
                        io.waits("vector", [tpe, tx, toks.get(("otile", i))])
                        v.tensor_tensor(oti[i][:], wop[i][:], xei[i][:], ALU.add)
                        tev = io.mark("vector")
                        toks[("xev", i)] = tev
                        dstt = os_
                    toks[("wo_ev", i)] = tev
                    io.wait("sync", tev)
                    toks[("otile", i)] = io.dma(
                        "sync", dstt[r0 + 128 * tc:r0 + 128 * (tc + 1), 512 * eb:512 * (eb + 1)],
                        oti[i][:], ("ot", i))
                toks[("woslot", eb % 2)] = io.now("pe")
                if eb + 2 < 4:
                    issue_wo(eb + 2)
            toks[("partial_w", seq)] = [io.now_ch(("ot", 0)), io.now_ch(("ot", 1))]
    io.barrier()


def emit_ffn_seq(io, toks, seq, T):
    nc = io.nc
    t, v, s, g, y = nc.tensor, nc.vector, nc.scalar, nc.gpsimd, nc.sync
    xs, ps, os_ = T["xs"], T["ps"], T["os"]
    NH1 = 22

    esq = contextlib.ExitStack()
    with esq:
        xn = sb(esq, nc, "xn_f", [128, NKC * FT], BF16)
        wgs = [sb(esq, nc, f"wgs{i}", [128, NKC * 128], BF16) for i in range(2)]
        wus = [sb(esq, nc, f"wus{i}", [128, NKC * 128], BF16) for i in range(2)]
        sdb = sb(esq, nc, "sdb", [128, D], F32)

        def issue_fc(fc):
            p = fc % 2
            io.waits("gpsimd", [toks.get(("wgu", p)), T["tffnw"]])
            a = io.dma("gpsimd", wgs[p][:].rearrange("p (c f) -> p c f", c=NKC),
                       bass.AP(T["wgB"], 128 * fc,
                               [[FF, 128], [128 * FF, NKC], [1, 128]]), ("wg", p))
            b = io.dma("gpsimd", wus[p][:].rearrange("p (c f) -> p c f", c=NKC),
                       bass.AP(T["wuB"], 128 * fc,
                               [[FF, 128], [128 * FF, NKC], [1, 128]]), ("wg", p),
                       burst=True)
            return [a, b]

        def load_xn(stg, war=None):
            xh = NKC // 2
            io.waits("sync", [toks[("xnf", seq)], toks.get(("xnst", 0)), war])
            r = [io.dma("sync", xn[:, 0:xh * FT].rearrange("p (c t) -> p c t", c=xh),
                        bass.AP(T["xnF"], seq * D * S + stg * FT,
                                [[S, 128], [128 * S, xh], [1, FT]]), ("xnst", 0))]
            io.waits("scalar", [toks[("xnf", seq)], toks.get(("xnst", 1)), war])
            r.append(io.dma("scalar",
                            xn[:, xh * FT:].rearrange("p (c t) -> p c t", c=xh),
                            bass.AP(T["xnF"], seq * D * S + xh * 128 * S + stg * FT,
                                    [[S, 128], [128 * S, xh], [1, FT]]), ("xnst", 1)))
            toks[("xnst", 0)] = r[0]
            toks[("xnst", 1)] = r[1]
            return r

        pre = {"t_xn": load_xn(0), "twg0": issue_fc(0)}
        for stg in range(NFST):
            t0 = seq * S + stg * FT
            esh = contextlib.ExitStack()
            emit_ffn_stage(io, toks, seq, T, stg, t0, esh, nc, xn, wgs, wus,
                           issue_fc, load_xn, pre, sdb)
    io.barrier()


def emit_ffn_stage(io, toks, seq, T, stg, t0, esh, nc, xn, wgs, wus,
                   issue_fc, load_xn, pre, sdb):
    t, v, s, g, y = nc.tensor, nc.vector, nc.scalar, nc.gpsimd, nc.sync
    xs, ps, os_ = T["xs"], T["ps"], T["os"]
    NH1 = 22
    if True:
        with esh:
            hh = sb(esh, nc, "hh", [128, NFC * FT], BF16)
            wdh = [sb(esh, nc, f"wdh{i}", [128, NH1 * 512], BF16) for i in range(2)]

            def issue_wd(g):
                eb, hf = g // 2, g % 2
                sl = g % 2
                io.waits("gpsimd", [toks.get(("wdslot", sl)), T["tffnw"]])
                toks[("wdld", g)] = io.dma(
                    "gpsimd", wdh[sl][:].rearrange("p (c e) -> p c e", c=NH1),
                    bass.AP(T["wdB"], 512 * eb + NH1 * hf * 128 * D,
                            [[D, 128], [128 * D, NH1], [1, 512]]), ("wd", sl))

            esl = contextlib.ExitStack()
            with esl:
                esg = contextlib.ExitStack()
                with esg:
                    gts0 = sb(esg, nc, "gts0", [128, 512], F32)
                    gts1 = sb(esg, nc, "gts1", [128, 512], F32)
                    pg0 = pt(esg, nc, "pg0", [128, 512])
                    pg1 = pt(esg, nc, "pg1", [128, 512])
                    pu0 = pt(esg, nc, "pu0", [128, 512])
                    pu1 = pt(esg, nc, "pu1", [128, 512])
                    gts, pgs, pus = [gts0, gts1], [pg0, pg1], [pu0, pu1]

                    twg = {0: pre["twg0"]}
                    issue_wd(0)
                    issue_wd(1)
                    if stg == 0:
                        io.waits("sync", [toks.get("sdb_w")])
                        toks["tsdb"] = io.dma(
                            "sync", sdb[:],
                            bass.AP(T["sdi"], 0, [[0, 128], [1, D]]), "consts")
                    io.waits("tensor", pre["t_xn"])
                    tdv = None
                    for fc in range(NFC):
                        p = fc % 2
                        if fc + 1 < NFC:
                            toks[("wgu", (fc + 1) % 2)] = io.now("pe")
                            twg[fc + 1] = issue_fc(fc + 1)
                        for tb in range(FT // 512):
                            io.waits("tensor", twg[fc] + [toks.get(("pgu", tb))])
                            for kc in range(NKC):
                                t.matmul(pgs[tb][:], wgs[p][:, 128 * kc:128 * (kc + 1)],
                                         xn[:, kc * FT + 512 * tb:kc * FT + 512 * (tb + 1)],
                                         start=(kc == 0), stop=(kc == NKC - 1))
                            for kc in range(NKC):
                                mm = t.matmul(pus[tb][:], wus[p][:, 128 * kc:128 * (kc + 1)],
                                              xn[:, kc * FT + 512 * tb:kc * FT + 512 * (tb + 1)],
                                              start=(kc == 0), stop=(kc == NKC - 1))
                            tpe = io.mark_mm(mm)
                            io.waits("scalar", [tpe, toks.get(("gts", tb))])
                            s.activation(gts[tb][:], pgs[tb][:], AF.Silu, bias=0.0,
                                         scale=T["sgc"][:, fc:fc + 1])
                            tac = io.mark("scalar")
                            io.wait("vector", tac)
                            v.scalar_tensor_tensor(
                                hh[:, fc * FT + 512 * tb:fc * FT + 512 * (tb + 1)],
                                pus[tb][:], T["suc"][:, fc:fc + 1], gts[tb][:],
                                ALU.mult, ALU.mult)
                            tdv = io.mark("vector")
                            toks[("pgu", tb)] = tdv
                            toks[("gts", tb)] = tdv
                    thh = tdv
            io.barrier()

            esd = contextlib.ExitStack()
            with esd:
                pt0 = sb(esd, nc, "ptile0", [128, 512], F32)
                pt1 = sb(esd, nc, "ptile1", [128, 512], F32)
                xe0 = sb(esd, nc, "xev0", [128, 512], F32)
                xe1 = sb(esd, nc, "xev1", [128, 512], F32)
                ot0 = sb(esd, nc, "otile0", [128, 512], F32)
                ot1 = sb(esd, nc, "otile1", [128, 512], F32)
                NTCS = FT // 128
                pds = [pt(esd, nc, f"pdt{i}", [128, 512]) for i in range(NTCS)]
                pts, xes, ots = [pt0, pt1], [xe0, xe1], [ot0, ot1]
                tsdb = toks["tsdb"]
                if stg + 1 < NFST:
                    toks[("wgu", 0)] = io.now("pe")
                    toks[("wgu", 1)] = io.now("pe")
                    pre["t_xn"] = load_xn(stg + 1, war=thh)
                    pre["twg0"] = issue_fc(0)
                for eb in range(4):
                    io.waits("tensor", [toks[("wdld", 2 * eb)], thh])
                    for tc in range(NTCS):
                        io.waits("tensor", [toks.get(("pdev", tc))])
                        for fc in range(NH1):
                            mm = t.matmul(pds[tc][:],
                                          hh[:, fc * FT + 128 * tc:fc * FT + 128 * (tc + 1)],
                                          wdh[0][:, 512 * fc:512 * (fc + 1)],
                                          start=(fc == 0), stop=False)
                    toks[("wdslot", 0)] = io.mark_mm(mm)
                    if eb + 1 < 4:
                        issue_wd(2 * eb + 2)
                    io.waits("tensor", [toks[("wdld", 2 * eb + 1)]])
                    for tc in range(NTCS):
                        i = tc % 2
                        trow = t0 + 128 * tc
                        io.waits("sync", [toks.get(("ptile", i)), toks.get(("xev", i)),
                                          toks.get(("partial_w", seq))])
                        tp_ = io.dma("sync", pts[i][:],
                                     ps[trow:trow + 128, 512 * eb:512 * (eb + 1)], ("pt", i))
                        tx_ = io.dma("sync", xes[i][:],
                                     xs[trow:trow + 128, 512 * eb:512 * (eb + 1)], ("xe", i))
                        for fc in range(NH1, NFC):
                            mm = t.matmul(pds[tc][:],
                                          hh[:, fc * FT + 128 * tc:fc * FT + 128 * (tc + 1)],
                                          wdh[1][:, 512 * (fc - NH1):512 * (fc - NH1) + 512],
                                          start=False, stop=(fc == NFC - 1))
                        tpe = io.mark_mm(mm)
                        io.waits("vector", [tpe, tp_, toks.get(("ot_po", i)),
                                            toks.get(("otile", i)), tsdb])
                        v.tensor_tensor(ots[i][:], pds[tc][:],
                                        sdb[:, 512 * eb:512 * (eb + 1)], ALU.mult)
                        v.drain()
                        v.tensor_tensor(ots[i][:], ots[i][:], pts[i][:], ALU.add)
                        tdv = io.mark("vector")
                        toks[("ptile", i)] = tdv
                        toks[("pdev", tc)] = tdv
                        io.waits("gpsimd", [tdv, tx_])
                        g.tensor_tensor(ots[i][:], ots[i][:], xes[i][:], ALU.add)
                        tpo = io.mark("gpsimd")
                        toks[("ot_po", i)] = tpo
                        toks[("xev", i)] = tpo
                        io.wait("sync", tpo)
                        toks[("otile", i)] = io.dma(
                            "sync", os_[trow:trow + 128, 512 * eb:512 * (eb + 1)], ots[i][:],
                            ("ot", i))
                    toks[("wdslot", 1)] = io.now("pe")
                    if eb + 1 < 4:
                        issue_wd(2 * eb + 3)
        io.barrier()


def build_program():
    nc = bass.Bass("TRN2", target_bir_lowering=False)

    x = nc.dram_tensor("x", [NSEQ, S, D], F32, kind="ExternalInput")
    out = nc.dram_tensor("out", [NSEQ, S, D], F32, kind="ExternalOutput")
    wqT = nc.dram_tensor("wqT", [D, D], F32, kind="ExternalInput")
    wkT = nc.dram_tensor("wkT", [D, D], F32, kind="ExternalInput")
    wvT = nc.dram_tensor("wvT", [D, D], F32, kind="ExternalInput")
    woT = nc.dram_tensor("woT", [D, D], F32, kind="ExternalInput")
    U8 = mybir.dt.uint8
    wgT = nc.dram_tensor("wgT", [D, FF], U8, kind="ExternalInput")
    wuT = nc.dram_tensor("wuT", [D, FF], U8, kind="ExternalInput")
    wdT = nc.dram_tensor("wdT", [FF, D], U8, kind="ExternalInput")
    memT = nc.dram_tensor("memT", [H, HD, M, HD], F32, kind="ExternalInput")
    mnT = nc.dram_tensor("mnT", [H, HD, M], F32, kind="ExternalInput")
    gatei = nc.dram_tensor("gatei", [1, H], F32, kind="ExternalInput")
    sgi = nc.dram_tensor("sgi", [128, NFC], F32, kind="ExternalInput")
    sui = nc.dram_tensor("sui", [128, NFC], F32, kind="ExternalInput")
    sdi = nc.dram_tensor("sdi", [1, D], F32, kind="ExternalInput")
    maski = nc.dram_tensor("maski", [128, 128], F32, kind="ExternalInput")
    identi = nc.dram_tensor("identi", [128, 128], BF16, kind="ExternalInput")

    wgB = nc.dram_tensor("wgB", [D, FF], BF16, kind="Internal")
    wuB = nc.dram_tensor("wuB", [D, FF], BF16, kind="Internal")
    wdB = nc.dram_tensor("wdB", [FF, D], BF16, kind="Internal")
    partial = nc.dram_tensor("partial", [NSEQ, S, D], F32, kind="Internal")
    xnF = nc.dram_tensor("xnF", [NSEQ, D, S], BF16, kind="Internal")

    est = contextlib.ExitStack()
    dma_pool = [f"d{i:02d}" for i in range(32)]
    sem_names = ["pe", "dv", "ac", "po", "bar"] + dma_pool
    sems = {n: est.enter_context(nc.semaphore(n)) for n in sem_names}
    io = Ctx(nc, sems, dma_pool)
    t, v, s, g, y = nc.tensor, nc.vector, nc.scalar, nc.gpsimd, nc.sync
    toks = {}

    with est:
        ident = sb(est, nc, "ident", [128, 128], BF16)
        mask = sb(est, nc, "mask", [128, 128], F32)
        ones_bf = sb(est, nc, "ones_bf", [128, 1], BF16)
        onesr_f = sb(est, nc, "onesr_f", [1, 128], F32)
        ones1_f = sb(est, nc, "ones1_f", [1, 1], F32)
        onesc_f = sb(est, nc, "onesc_f", [128, 1], F32)
        epsb = sb(est, nc, "epsb", [128, 1], F32)
        gsb = sb(est, nc, "gsb", [1, H], F32)
        gsig = sb(est, nc, "gsig", [1, H], F32)
        g1b = sb(est, nc, "g1b", [128, H], F32)
        mnsb = sb(est, nc, "mnsb", [128, H * M], F32)
        mnb16 = sb(est, nc, "mnb16", [128, H * M], BF16)
        ne4 = sb(est, nc, "ne4", [1, M], F32)
        sgc = sb(est, nc, "sgc", [128, NFC], F32)
        suc = sb(est, nc, "suc", [128, NFC], F32)
        smp = sb(est, nc, "smp", [1, 128], F32)
        cvb_i = [sb(est, nc, f"cvb_i{i}", [128, 512], BF16) for i in range(2)]
        cvb_o = [sb(est, nc, f"cvb_o{i}", [128, 512], BF16) for i in range(2)]
        est.enter_context(nc.Block())
        if True:
            T = dict(xs=x.ap().rearrange("b s d -> (b s) d"),
                     os=out.ap().rearrange("b s d -> (b s) d"),
                     ps=partial.ap().rearrange("b s d -> (b s) d"),
                     wqT=wqT, wkT=wkT, wvT=wvT, woT=woT, wgB=wgB, wuB=wuB, wdB=wdB,
                     memT=memT, ident=ident, mask=mask, ones_bf=ones_bf,
                     onesr_f=onesr_f, ones1_f=ones1_f, epsb=epsb, gsig=gsig,
                     g1b=g1b, mnb16=mnb16, ne4=ne4, smp=smp, sgc=sgc, suc=suc,
                     xnF=xnF, tffnw=None, cv_step=lambda n=1: None)

            tld = [io.dma("sync", ident[:], identi.ap(), "consts"),
                   io.dma("sync", mask[:], maski.ap(), "consts"),
                   io.dma("sync", gsb[:], gatei.ap(), "consts"),
                   io.dma("sync", sgc[:], sgi.ap(), "consts"),
                   io.dma("sync", suc[:], sui.ap(), "consts"),
                   io.dma("sync", mnsb[:], mnT.ap().rearrange("h d m -> d h m"), "consts")]
            g.memset(ones_bf[:], 1.0)
            g.memset(onesr_f[:], 1.0)
            g.memset(ones1_f[:], 1.0)
            g.memset(onesc_f[:], 1.0)
            g.memset(epsb[:], LN_EPS)
            tpo = io.mark("gpsimd")
            io.waits("scalar", tld + [tpo])
            s.activation(gsig[:], gsb[:], AF.Sigmoid, bias=0.0, scale=1.0)
            tac = io.mark("scalar")
            io.waits("vector", [tac] + tld)
            v.tensor_scalar(smp[0:1, 0:H], gsig[:], -1.0, 1.0, ALU.mult, ALU.add)
            v.tensor_copy(mnb16[:], mnsb[:])
            tdv = io.mark("vector")

            esp = contextlib.ExitStack()
            with esp:
                pbc = pt(esp, nc, "pbc", [128, 512])
                io.wait("tensor", tdv)
                mm = t.matmul(pbc[:, 0:H], onesr_f[:], smp[0:1, 0:H], start=True, stop=True)
                tpe = io.mark_mm(mm)
                io.wait("vector", tpe)
                v.tensor_copy(g1b[:], pbc[:, 0:H])
                tg1b = io.mark("vector")
                io.waits("tensor", [tg1b])
                mm = t.matmul(pbc[0:1, 64:128], onesc_f[:], mnsb[:], start=True, stop=True)
                tpe2 = io.mark_mm(mm)
                io.wait("vector", tpe2)
                v.tensor_copy(smp[0:1, 16:80], pbc[0:1, 64:128])
                v.drain()
                v.tensor_reduce(ne4[:], bass.AP(ap_of(smp), 16, [[128, 1], [1, 4], [4, 16]]),
                                AX.X, ALU.add)
                v.drain()
                v.tensor_single_scalar(ne4[:], ne4[:], EPS, ALU.is_ge)
                tne = io.mark("vector")

            io.barrier()

            cvq = []
            if DO_FFN:
                tiles = []
                for mat, dst in ((wgT, wgB), (wuT, wuB)):
                    for rt in range(16):
                        for st_ in range(11):
                            tiles.append((mat, dst, rt, 512 * st_))
                for rt in range(NFC):
                    for st_ in range(4):
                        tiles.append((wdT, wdB, rt, 512 * st_))
                cvst = {"load": 0, "proc": 0, "tin": {}}

                def cv_load(i):
                    mat, dst, rt, c0 = tiles[i]
                    par = i % 2
                    io.waits("gpsimd", [toks.get(("cvi_war", par))])
                    cvst["tin"][i] = io.dma(
                        "gpsimd", cvb_i[par][:],
                        mat.ap()[128 * rt:128 * (rt + 1), c0:c0 + 512], ("cvi", par))

                def cv_proc(i):
                    mat, dst, rt, c0 = tiles[i]
                    par = i % 2
                    io.waits("scalar", [cvst["tin"].pop(i), toks.get(("cvo_w", par))])
                    s.activation(cvb_o[par][:], cvb_i[par][:], AF.Copy,
                                 bias=-128.0, scale=1.0)
                    tcv = io.mark("scalar")
                    toks[("cvi_war", par)] = tcv
                    io.wait("sync", tcv)
                    toks[("cvo_w", par)] = io.dma(
                        "sync", dst.ap()[128 * rt:128 * (rt + 1), c0:c0 + 512],
                        cvb_o[par][:], ("cvo", par))

                def cv_tick():
                    while cvst["load"] < min(cvst["proc"] + 2, len(tiles)):
                        cv_load(cvst["load"])
                        cvst["load"] += 1
                    if cvst["proc"] < cvst["load"]:
                        cv_proc(cvst["proc"])
                        cvst["proc"] += 1
                        return True
                    return False

                for _ in tiles:
                    cvq.append(cv_tick)

            def cv_step(n=1):
                for _ in range(n):
                    if cvq:
                        cvq.pop(0)()
            T["cv_step"] = cv_step
            T["cvq"] = cvq
            T["sdi"] = sdi

            for seq in range(NSEQ):
                if DO_ATTN:
                    emit_attn_seq(io, toks, seq, T)
                while cvq:
                    cv_step()
                if T.get("tffnw") is None:
                    T["tffnw"] = [io.now_ch(("cvo", 0)), io.now_ch(("cvo", 1)),
                                  io.now_ch(("cvi", 0)), io.now_ch(("cvi", 1))]
                if DO_FFN:
                    emit_ffn_seq(io, toks, seq, T)

            io.barrier()
            for ch, q in io.chan.items():
                if io.ct[q] > 0:
                    y.wait_ge(sems[q], io.ct[q])

    nc.finalize()
    return nc


_nc_cache = {}


def _get_program():
    key = (DO_ATTN, DO_FFN)
    if key not in _nc_cache:
        _nc_cache[key] = build_program()
    return _nc_cache[key]


def kernel(L0_attn_hidden_states, L0_attn_memories, L0_attn_memory_norms,
           L0_attn_w_q, L0_attn_w_k, L0_attn_w_v, L0_attn_w_o, L0_attn_gate,
           L0_ffn_wq_gate, L0_ffn_s_gate, L0_ffn_wq_up, L0_ffn_s_up,
           L0_ffn_wq_down, L0_ffn_s_down):
    x = np.asarray(L0_attn_hidden_states, np.float32)
    co = np.ascontiguousarray
    shared = {
        "wqT": co(np.asarray(L0_attn_w_q, np.float32).T),
        "wkT": co(np.asarray(L0_attn_w_k, np.float32).T),
        "wvT": co(np.asarray(L0_attn_w_v, np.float32).T),
        "woT": co(np.asarray(L0_attn_w_o, np.float32).T),
        "wgT": co(np.asarray(L0_ffn_wq_gate, np.int32).T.astype(np.uint8)),
        "wuT": co(np.asarray(L0_ffn_wq_up, np.int32).T.astype(np.uint8)),
        "wdT": co(np.asarray(L0_ffn_wq_down, np.int32).T.astype(np.uint8)),
        "memT": co(np.transpose(np.asarray(L0_attn_memories, np.float32), (1, 2, 0, 3))),
        "mnT": co(np.transpose(np.asarray(L0_attn_memory_norms, np.float32), (1, 2, 0))),
        "gatei": co(np.asarray(L0_attn_gate, np.float32).reshape(1, H)),
        "sgi": co(np.asarray(L0_ffn_s_gate, np.float32).reshape(NFC, 128).T),
        "sui": co(np.asarray(L0_ffn_s_up, np.float32).reshape(NFC, 128).T),
        "sdi": co(np.asarray(L0_ffn_s_down, np.float32).reshape(1, D)),
        "maski": co(np.triu(np.ones((128, 128), np.float32))),
        "identi": np.eye(128, dtype=BFD),
    }
    in_maps = []
    for c in range(8):
        m = dict(shared)
        m["x"] = co(x[NSEQ * c:NSEQ * (c + 1)])
        in_maps.append(m)
    nc = _get_program()
    res = run_bass_kernel_spmd(nc, in_maps, core_ids=list(range(8)))
    return np.concatenate([r["out"] for r in res.results], axis=0)


if __name__ == "__main__":
    nc = _get_program()
    print("instructions:", len(list(nc.all_instructions())))
